# version 25
# baseline (speedup 1.0000x reference)
import numpy as np
import ml_dtypes
import concourse.bass as bass
import concourse.mybir as mybir
from concourse.bass_utils import run_bass_kernel_spmd
from contextlib import ExitStack

F32 = mybir.dt.float32
BF16 = mybir.dt.bfloat16
AF = mybir.ActivationFunctionType
ALU = mybir.AluOpType
AX = mybir.AxisListType

B, L, D = 2, 16384, 1024
TPC = 4096
NH = 16
NE, DHID = 8, 2048
SPAN = 512
NSPAN = TPC // SPAN
RANGE = 1024
NRANGE = TPC // RANGE
LOOK = 12288
LCH = 128
LTILE = 4
NLT = LOOK // (LCH * LTILE)
NCH = LOOK // LCH
LN_EPS = 1e-5
BIG = 1e30
R1024 = 1.0 / 1024.0

ENGINES = ("pe", "dv", "ac", "po", "sp")
STAGE = 2


class Plan:
    def __init__(self):
        self.q = {e: [] for e in ENGINES}
        self.cnt = {}
        self.producer = {}
        self.waited = {e: {} for e in ENGINES}
        self.cut = None

    AUTOSYNC = ("dv", "ac", "po")

    def op(self, eng, fn, waits=(), incs=(), is_dma=False):
        auto = (fn is not None) and (not is_dma) and eng in self.AUTOSYNC
        if auto:
            waits = tuple(waits) + ((eng, self.cnt.get(eng, 0)),)
            incs = tuple(incs) + ((eng, 1),)
        fw = []
        for (s, v) in waits:
            if v is None or v <= 0:
                continue
            if self.waited[eng].get(s, 0) >= v:
                continue
            fw.append((s, v))
            self.waited[eng][s] = v
        for (s, amt) in incs:
            self.cnt[s] = self.cnt.get(s, 0) + amt
            p = self.producer.get(s)
            if p is None:
                self.producer[s] = eng
            elif p != eng:
                self.producer[s] = "-"
        self.q[eng].append((tuple(fw), fn, tuple(incs)))

    def mst(self, eng, waits=()):
        assert not waits
        if eng in self.AUTOSYNC:
            return self.cnt[eng]
        q = self.q[eng]
        assert q and q[-1][1] is not None, "mst needs a preceding real op"
        w, fn, incs = q[-1]
        self.cnt[eng] = self.cnt.get(eng, 0) + 1
        q[-1] = (w, fn, incs + ((eng, 1),))
        return self.cnt[eng]

    def mark_cut(self):
        self.cut = {e: len(self.q[e]) for e in ENGINES}

    def replay(self, eng, handle, sems, phase):
        lo = self.cut[eng] if phase == 1 else 0
        hi = self.cut[eng] if phase == 0 else len(self.q[eng])
        for (waits, fn, incs) in self.q[eng][lo:hi]:
            for (s, v) in waits:
                handle.wait_ge(sems[s], v)
            inst = fn(handle) if fn is not None else None
            if incs:
                if inst is None:
                    inst = handle.nop()
                for (s, a) in incs:
                    inst = inst.then_inc(sems[s], a)


def build_program(stage=2):
    att_spans = NSPAN
    do_look = True
    do_moe = (stage == 2)
    if stage == "setup":
        att_spans, do_look = 0, False
    elif stage == "look":
        att_spans = 0
    elif isinstance(stage, tuple) and stage[0] == "spans":
        att_spans = stage[1]
    nc = bass.Bass("TRN2", target_bir_lowering=False)

    xin = nc.dram_tensor("xin", [TPC, D], F32, kind="ExternalInput").ap()
    xlk = nc.dram_tensor("xlk", [LOOK, D], F32, kind="ExternalInput").ap()
    wq = nc.dram_tensor("wq", [D, D], F32, kind="ExternalInput").ap()
    wv = nc.dram_tensor("wv", [D, D], F32, kind="ExternalInput").ap()
    wg = nc.dram_tensor("wg", [D, D], F32, kind="ExternalInput").ap()
    wo = nc.dram_tensor("wo", [D, D], F32, kind="ExternalInput").ap()
    dec = nc.dram_tensor("dec", [NH, 2, 1], F32, kind="ExternalInput").ap()
    lng = nc.dram_tensor("lng", [D], F32, kind="ExternalInput").ap()
    lnb = nc.dram_tensor("lnb", [D], F32, kind="ExternalInput").ap()
    gwt = nc.dram_tensor("gwt", [D, NE], F32, kind="ExternalInput").ap()
    w1 = nc.dram_tensor("w1", [NE, D, DHID], F32, kind="ExternalInput").ap()
    b1 = nc.dram_tensor("b1", [NE, DHID], F32, kind="ExternalInput").ap()
    w2 = nc.dram_tensor("w2", [NE, DHID, D], F32, kind="ExternalInput").ap()
    b2 = nc.dram_tensor("b2", [NE, D], F32, kind="ExternalInput").ap()
    idf = nc.dram_tensor("idf", [128, 128], F32, kind="ExternalInput").ap()
    idb = nc.dram_tensor("idb", [128, 128], BF16, kind="ExternalInput").ap()
    erow = nc.dram_tensor("erow", [NH, LCH], F32, kind="ExternalInput").ap()
    out = nc.dram_tensor("out", [TPC, D], F32, kind="ExternalOutput").ap()

    x1scr = nc.dram_tensor("x1scr", [TPC, D], F32, kind="Internal").ap()
    x1b = nc.dram_tensor("x1b", [TPC, D], BF16, kind="Internal").ap()
    wtscr = {nm: nc.dram_tensor("wtscr_%s" % nm, [D, D], BF16,
                                 kind="Internal").ap()
             for nm in ("q", "v", "g", "o")}

    xin3 = xin.rearrange("(a p) d -> p a d", p=128)
    out3 = out.rearrange("(a p) d -> p a d", p=128)
    x1s3 = x1scr.rearrange("(a p) d -> p a d", p=128)
    xlk3 = xlk.rearrange("(c p) d -> p c d", p=128)

    sems = {}
    P = Plan()

    with ExitStack() as outer:
        semnames = (["dc", "db1", "dw", "dmeb", "dr"]
                    + ["dwc_%s" % n for n in ("q", "v", "g", "o")]
                    + ["dxc%d" % i for i in range(4)]
                    + ["dwt_%s" % n for n in ("q", "v", "g", "o")]
                    + ["dx%d" % i for i in range(8)]
                    + ["dla", "dlb", "ds0", "ds1", "do0", "do1"]
                    + ["dwb%d" % i for i in range(3)]
                    + ["drb%d" % i for i in range(4)]
                    + ["pe", "dv", "ac", "po", "sp"])
        for s in semnames:
            sems[s] = outer.enter_context(nc.semaphore(s))

        def sb(stack, name, shape, dt):
            return stack.enter_context(nc.sbuf_tensor(name, shape, dt))

        idfs = sb(outer, "idfs", [128, 128], F32)
        idbs = sb(outer, "idbs", [128, 128], BF16)
        decs = sb(outer, "decs", [NH, 2], F32)
        lamc = sb(outer, "lamc", [NH, 1], F32)
        llc = sb(outer, "llc", [NH, 1], F32)
        lamr = sb(outer, "lamr", [1, NH], F32)
        lamD = sb(outer, "lamD", [128, 8], F32)
        sinit = sb(outer, "sinit", [128, 8], F32)
        scur = sb(outer, "scur", [128, 8], F32)
        lngc = sb(outer, "lngc", [128, 8], F32)
        lnbc = sb(outer, "lnbc", [128, 8], F32)
        ln8 = sb(outer, "ln8", [8, 256], F32)
        onesc = sb(outer, "onesc", [128, 1], F32)
        onesr = sb(outer, "onesr", [1, 128], F32)
        erows = sb(outer, "erows", [NH, LCH], F32)
        b1s2 = sb(outer, "b1s2", [NE, 512], F32)
        b1T = sb(outer, "b1T", [128, 128], F32)
        b2b = sb(outer, "b2b", [NE, D], BF16)
        gwb = sb(outer, "gwb", [128, 64], BF16)
        rowb = sb(outer, "rowb", [1, 3 * SPAN], F32)
        rcol = sb(outer, "rcol", [128, 8], F32)
        rcol2 = sb(outer, "rcol2", [128, 8], F32)

        ps = [outer.enter_context(nc.psum_tensor("ps%d" % i, [128, 512], F32))
              for i in range(8)]
        bankf = {i: ("dv", 0) for i in range(8)}

        def dma(eng, dst, src, s, waits=(), transpose=False):
            if transpose:
                fn = lambda h, d=dst, r=src: h.dma_start(out=d, in_=r, transpose=True)
            else:
                fn = lambda h, d=dst, r=src: h.dma_start(out=d, in_=r)
            P.op(eng, fn, waits=waits, incs=((s, 16),), is_dma=True)
            return P.cnt[s]

        dma("ac", idfs[:], idf, "dc")
        dma("ac", idbs[:], idb, "dc")
        dma("ac", decs[:], dec.rearrange("h a b -> h (a b)"), "dc")
        dma("ac", ln8[:, 0:128], lng.rearrange("(a b) -> a b", b=128), "dc")
        dma("ac", ln8[:, 128:256], lnb.rearrange("(a b) -> a b", b=128), "dc")
        v_dc = dma("ac", erows[:], erow, "dc")
        v_idf = v_dec = v_ln = v_er = v_dc
        dma("po", b2b[:], b2, "dmeb")
        v_meb = dma("po", gwb[:].rearrange("p (kg e) -> p kg e", kg=8),
                    gwt.rearrange("(kg p) e -> p kg e", p=128), "dmeb")
        v_b2 = v_gw = v_meb

        P.op("dv", lambda h: h.memset(onesc[:], 1.0))
        P.op("dv", lambda h: h.memset(onesr[:], 1.0))
        P.op("dv", lambda h: h.tensor_scalar_max(lamc[:], decs[:, 1:2], 1e-30),
             waits=(("dc", v_dec),))
        v_lamc = P.mst("dv")
        P.op("ac", lambda h: h.activation(llc[:], lamc[:], AF.Ln),
             waits=(("dv", v_lamc),))
        v_llc = P.mst("ac")
        P.op("pe", lambda h: h.transpose(ps[6][0:1, 0:NH], lamc[:], idfs[0:NH, 0:NH]),
             waits=(("dv", v_lamc), ("dc", v_idf)))
        v_p = P.mst("pe")
        P.op("dv", lambda h: h.tensor_copy(lamr[:], ps[6][0:1, 0:NH]),
             waits=(("pe", v_p),))
        v_lamr = P.mst("dv")
        P.op("pe", lambda h: h.matmul(ps[6][:, 0:NH], onesr[0:1, :], lamr[0:1, :],
                                      start=True, stop=True),
             waits=(("dv", v_lamr),))
        v_p = P.mst("pe")
        for g in range(8):
            P.op("dv", lambda h, g=g: h.tensor_copy(
                lamD[0:64, g:g + 1], ps[6][0:64, 2 * g:2 * g + 1]),
                waits=(("pe", v_p),))
            P.op("dv", lambda h, g=g: h.tensor_copy(
                lamD[64:128, g:g + 1], ps[6][64:128, 2 * g + 1:2 * g + 2]))
        v_lamD = P.mst("dv")
        bankf[6] = ("dv", v_lamD)
        P.op("pe", lambda h: h.transpose(ps[7][:, 0:8], ln8[:, 0:128],
                                         idfs[0:8, 0:8]), waits=(("dc", v_ln),))
        P.op("pe", lambda h: h.transpose(ps[7][:, 8:16], ln8[:, 128:256],
                                         idfs[0:8, 0:8]))
        v_p = P.mst("pe")
        P.op("dv", lambda h: h.tensor_copy(lngc[:], ps[7][:, 0:8]),
             waits=(("pe", v_p),))
        P.op("dv", lambda h: h.tensor_copy(lnbc[:], ps[7][:, 8:16]))
        P.mst("dv")
        for bc in range(4):
            v_b1 = dma("ac", b1s2[:], b1[:, bc * 512:(bc + 1) * 512], "db1",
                       waits=(("pe", P.cnt["pe"]),))
            for j in range(4):
                jg = bc * 4 + j
                P.op("pe", lambda h, j=j: h.transpose(
                    ps[7][:, 0:8], b1s2[0:NE, j * 128:(j + 1) * 128],
                    idfs[0:NE, 0:NE]),
                    waits=(("db1", v_b1), ("dv", P.cnt["dv"])))
                v_p = P.mst("pe")
                P.op("dv", lambda h, jg=jg: h.tensor_copy(
                    b1T[:, jg * 8:(jg + 1) * 8], ps[7][:, 0:8]),
                    waits=(("pe", v_p),))
                P.mst("dv")
        bankf[7] = ("dv", P.cnt["dv"])

        with ExitStack() as att:
            wT = {nm: sb(att, "w%sT" % nm, [128, 8 * 1024], BF16)
                  for nm in ("q", "v", "g", "o")}
            xtm = sb(att, "xtm", [128, 8 * 1024], F32)
            xfmb = sb(att, "xfmb", [128, 2 * 4096], BF16)
            xlt = sb(att, "xlt", [128, 2 * 4096], BF16)
            dcT = sb(att, "dcT", [NH, 4 * LCH], F32)
            ofc = sb(att, "ofc", [NH, 8], F32)
            dcbu = sb(att, "dcbu", [128, 2 * 64], BF16)
            xbarb = sb(att, "xbarb", [128, 128], BF16)
            stat = sb(att, "stat", [128, 8 * SPAN], F32)
            ysq = sb(att, "ysq", [128, 3 * 512], F32)
            ynt = sb(att, "ynt", [128, 512], F32)
            ynbb = sb(att, "ynbb", [128, 512], BF16)
            vsb = sb(att, "vsb", [128, 2 * 512], BF16)
            gtb = sb(att, "gtb", [128, 2 * 8 * SPAN], BF16)
            r1sq = sb(att, "r1sq", [128, 1024], F32)
            x1st = sb(att, "x1st", [128, 2 * 2048], F32)
            bcb = xlt[:, 0:2048].bitcast(F32)
            ynts = [ynt[:, 0:512], xlt[:, 2048:3072].bitcast(F32)]

            v_wt = {}

            wcv = {}

            def cast_weight_dram(nm, src):
                wcv[nm] = dma("po", wtscr[nm], src, "dwc_%s" % nm)

            def cast_weight(nm):
                for kg in range(8):
                    dma("sp", wT[nm][:, kg * 1024:(kg + 1) * 1024],
                        wtscr[nm][:, kg * 128:(kg + 1) * 128], "dwt_%s" % nm,
                        waits=(("dwc_%s" % nm, wcv[nm]),) if kg == 0 else (),
                        transpose=True)
                v_wt[nm] = P.cnt["dwt_%s" % nm]

            cast_weight_dram("v", wv)
            cast_weight_dram("q", wq)
            cast_weight_dram("g", wg)
            cast_weight_dram("o", wo)
            cast_weight("v")

            lk_done = []
            for t in range(NLT if do_look else 0):
                par = t % 2
                buf = xlt[:, par * 4096:(par + 1) * 4096]
                w = (("pe", lk_done[t - 2]),) if t >= 2 else ()
                eng, dsem = ("po", "dla") if par == 0 else ("po", "dlb")
                vld = dma(eng, buf.rearrange("p (c d) -> p c d", c=LTILE),
                          xlk3[:, t * LTILE:(t + 1) * LTILE, :], dsem, waits=w)
                for cc in range(LTILE):
                    c = t * LTILE + cc
                    off_c = float(LOOK - LCH - c * LCH)
                    P.op("dv", lambda h, i=cc, o=off_c: h.tensor_scalar_mul(
                        ofc[:, i:i + 1], llc[:], o),
                        waits=(("ac", v_llc),
                               ("pe", lk_done[t - 1] if t >= 1 else 0)))
                v_of = P.mst("dv")
                for cc in range(LTILE):
                    i = par * 4 + cc
                    wq_ = [("dv", v_of), ("dc", v_er)]
                    if cc == 0:
                        wq_.append(("pe", lk_done[t - 1] if t >= 1 else 0))
                    P.op("ac", lambda h, cc=cc: h.activation(
                        dcT[:, cc * LCH:(cc + 1) * LCH], erows[:], AF.Exp,
                        bias=ofc[:, cc:cc + 1], scale=llc[:, 0:1]),
                        waits=tuple(wq_))
                v_dct = P.mst("ac")
                for cc in range(LTILE):
                    i = par * 4 + cc
                    P.op("pe", lambda h, i=i, cc=cc: h.transpose(
                        ps[6][:, i * 16:(i + 1) * 16],
                        dcT[:, cc * LCH:(cc + 1) * LCH], idfs[0:NH, 0:NH]),
                        waits=(("ac", v_dct), bankf[6]))
                v_tp = P.mst("pe")
                P.op("dv", lambda h, par=par: h.tensor_copy(
                    dcbu[:, par * 64:(par + 1) * 64],
                    ps[6][:, par * 64:(par + 1) * 64]),
                    waits=(("pe", v_tp),))
                v_dcb = P.mst("dv")
                bankf[6] = ("dv", v_dcb)
                for cc in range(LTILE):
                    c = t * LTILE + cc
                    for half_ in range(2):
                        st = (c == 0)
                        w = ((dsem, vld), ("dv", v_dcb)) if (cc == 0 and half_ == 0) else ()
                        P.op("pe", lambda h, cc=cc, half_=half_, st=st, par=par, b=buf: h.matmul(
                            ps[5 - half_][0:NH, 0:512],
                            dcbu[:, par * 64 + cc * 16:par * 64 + (cc + 1) * 16],
                            b[:, cc * 1024 + half_ * 512:cc * 1024 + (half_ + 1) * 512],
                            start=st, stop=(c == NCH - 1),
                            skip_group_check=True),
                            waits=w)
                lk_done.append(P.mst("pe"))
            v_lkpe = lk_done[-1] if lk_done else 0
            if not do_look:
                P.op("dv", lambda h: h.memset(sinit[:], 0.0))
            if do_look:
                for half_ in range(2):
                    P.op("dv", lambda h, half_=half_: h.tensor_copy(
                        dcT[:, 0:512], ps[5 - half_][0:NH, 0:512]),
                        waits=(("pe", v_lkpe),))
                    v_xbt = P.mst("dv")
                    for k4 in range(4):
                        kg = half_ * 4 + k4
                        bank = 6 + (kg % 2)
                        P.op("pe", lambda h, k4=k4, bank=bank: h.transpose(
                            ps[bank][:, 0:16],
                            dcT[0:NH, k4 * 128:(k4 + 1) * 128], idfs[0:NH, 0:NH]),
                            waits=(("dv", v_xbt), bankf[bank]))
                        v_p = P.mst("pe")
                        P.op("dv", lambda h, kg=kg, bank=bank: h.tensor_copy(
                            xbarb[:, kg * 16:(kg + 1) * 16], ps[bank][:, 0:16]),
                            waits=(("pe", v_p),))
                        bankf[bank] = ("dv", P.mst("dv"))
                    if half_ == 0:
                        P.op("dv", lambda h: h.engine_nop() if False else h.nop(),
                             waits=(("pe", P.cnt["pe"]),))
            v_xbar = P.cnt["dv"]
            bankf[5] = ("dv", v_xbar)
            bankf[4] = ("dv", v_xbar)
            cast_weight("q")
            cast_weight("g")
            cast_weight("o")
            v_scopy = 0
            for og in range(8 if do_look else 0):
                for kg in range(8):
                    w = ()
                    if og == 0 and kg == 0:
                        w = (("dwt_v", v_wt["v"]), ("dv", v_xbar))
                    elif kg == 0:
                        w = (("dv", v_scopy),)
                    P.op("pe", lambda h, og=og, kg=kg: h.matmul(
                        ps[4][:, 0:16],
                        wT["v"][:, kg * 1024 + og * 128:kg * 1024 + (og + 1) * 128],
                        xbarb[:, kg * 16:(kg + 1) * 16],
                        start=(kg == 0), stop=(kg == 7)),
                        waits=w)
                v_p = P.mst("pe")
                P.op("dv", lambda h, og=og: h.tensor_copy(
                    sinit[0:64, og:og + 1], ps[4][0:64, 2 * og:2 * og + 1]),
                    waits=(("pe", v_p),))
                P.op("dv", lambda h, og=og: h.tensor_copy(
                    sinit[64:128, og:og + 1], ps[4][64:128, 2 * og + 1:2 * og + 2]))
                v_scopy = P.mst("dv")
            bankf[4] = ("dv", P.cnt["dv"])

            xld = {}
            slot_free = {i: ("dv", 0) for i in range(8)}
            x1o_do = {0: 0, 1: 0}
            v_x1_prev = [0]
            xf_free = {0: ("pe", 0), 1: ("pe", 0)}
            x1slot_ds = {0: 0, 1: 0}
            v_stats_prev = 0
            v_wo_prev = 0
            v_yn_prev = 0

            def load_tile(gt):
                slot = gt % 8
                v = dma("ac", xtm[:, slot * 1024:(slot + 1) * 1024]
                        .rearrange("p (a d) -> p a d", a=1),
                        xin3[:, gt:gt + 1, :], "dx%d" % slot,
                        waits=(slot_free[slot],))
                xld[gt] = ("dx%d" % slot, v)

            for gt in range(8 if att_spans else 0):
                load_tile(gt)

            v_xf_span = {}

            def emit_transposes(s_):
                sl = s_ % 2
                xfsl3_ = xfmb[:, sl * 4096:(sl + 1) * 4096]                    .rearrange("p (kg t) -> p kg t", kg=8)
                bidx = 0
                for half in range(2):
                    hs = 2 * s_ + half
                    for ttl in range(2):
                        gt = 2 * hs + ttl
                        slot = gt % 8
                        for kq in range(2):
                            bank = 4 + (bidx % 2)
                            bidx += 1
                            for j in range(4):
                                kg = kq * 4 + j
                                w = []
                                if j == 0:
                                    w = [xld[gt], bankf[bank]]
                                    if bidx <= 2:
                                        w.append(xf_free[sl])
                                P.op("pe", lambda h, slot=slot, kg=kg, j=j, bank=bank:
                                     h.transpose(
                                         ps[bank][:, j * 128:(j + 1) * 128],
                                         xtm[:, slot * 1024 + kg * 128:
                                             slot * 1024 + (kg + 1) * 128],
                                         idfs[:]),
                                     waits=tuple(w))
                            v_p = P.mst("pe")
                            P.op("dv", lambda h, kq=kq, half=half, ttl=ttl, bank=bank,
                                 x3=xfsl3_: h.tensor_copy(
                                     x3[:, kq * 4:(kq + 1) * 4,
                                        half * 256 + ttl * 128:half * 256 + ttl * 128 + 128],
                                     ps[bank][:, 0:512].rearrange("p (a t) -> p a t", a=4)),
                                 waits=(("pe", v_p),))
                            bankf[bank] = ("dv", P.mst("dv"))
                v_xf_span[s_] = ("dv", P.cnt["dv"])

            if att_spans:
                emit_transposes(0)

            front = {}
            vsp = [0]
            vscan_prev = {0: 0, 1: 0}

            def emit_front(s_, yn_emit=None):
                sl = s_ % 2
                xfsl = xfmb[:, sl * 4096:(sl + 1) * 4096]
                v_xf = v_xf_span[s_]
                F = dict(qmm=[0] * 8, vmm=[0] * 8, gmm=[0] * 8, y=[0] * 8,
                         silu=[0] * 8, sq=[0] * 8, s2mm=[0] * 8, yna=[0] * 8)
                front[s_] = F
                prevF = front.get(s_ - 1)

                def emit_stats(g, F=F):
                    w1_ = [("dv", F["y"][g])]
                    if g == 0:
                        w1_.append(bankf[6])
                    P.op("pe", lambda h, g=g: h.matmul(
                        ps[6][0:1, 0:512], onesc[:, 0:1],
                        stat[:, g * SPAN:(g + 1) * SPAN],
                        start=(g == 0), stop=(g == 7), skip_group_check=True),
                        waits=tuple(w1_))
                    w2_ = [("ac", F["sq"][g])]
                    if g == 0:
                        w2_.append(bankf[7])
                    P.op("pe", lambda h, g=g: h.matmul(
                        ps[7][0:1, 0:512], onesc[:, 0:1],
                        ysq[:, (g % 3) * 512:(g % 3 + 1) * 512],
                        start=(g == 0), stop=(g == 7), skip_group_check=True),
                        waits=tuple(w2_))
                    F["s2mm"][g] = P.mst("pe")

                for g in range(8):
                    if yn_emit is not None:
                        yn_emit(g)
                    for (wTn, boff, vlist) in (("q", 0, F["qmm"]),
                                               ("v", 2, F["vmm"]),
                                               ("g", 4, F["gmm"])):
                        bank = boff + (g % 2)
                        for kg in range(8):
                            w = ()
                            if kg == 0:
                                w = (v_xf, ("dwt_%s" % wTn, v_wt[wTn]), bankf[bank])
                            P.op("pe", lambda h, g=g, kg=kg, wTn=wTn, bank=bank, xfsl=xfsl:
                                 h.matmul(
                                     ps[bank][:, 0:512],
                                     wT[wTn][:, kg * 1024 + g * 128:kg * 1024 + (g + 1) * 128],
                                     xfsl[:, kg * 512:(kg + 1) * 512],
                                     start=(kg == 0), stop=(kg == 7)),
                                 waits=w)
                        vlist[g] = P.mst("pe")
                    vbank = 2 + (g % 2)
                    qbank = g % 2
                    init = sinit if s_ == 0 else scur
                    P.op("dv", lambda h, g=g, vbank=vbank: h.tensor_copy(
                        vsb[:, (g % 2) * 512:(g % 2 + 1) * 512],
                        ps[vbank][:, 0:512]),
                        waits=(("pe", F["vmm"][g]),))
                    v_vev = P.mst("dv")
                    bankf[vbank] = ("dv", v_vev)
                    swaits = [("pe", vsp[0])]
                    if prevF is not None:
                        swaits.append(("ac", prevF["sq"][g]))
                    P.op("dv", lambda h, g=g, init=init: h.tensor_tensor_scan(
                        stat[:, g * SPAN:(g + 1) * SPAN],
                        lamD[:, g:g + 1].broadcast_to([128, SPAN]),
                        vsb[:, (g % 2) * 512:(g % 2 + 1) * 512],
                        init[:, g:g + 1],
                        op0=ALU.mult, op1=ALU.add),
                        waits=tuple(swaits))
                    P.op("dv", lambda h, g=g: h.tensor_copy(
                        scur[:, g:g + 1],
                        stat[:, g * SPAN + SPAN - 1:g * SPAN + SPAN]))
                    v_scan = P.mst("dv")
                    vscan_prev[g % 2] = v_scan
                    P.op("dv", lambda h, g=g, qbank=qbank: h.tensor_tensor(
                        stat[:, g * SPAN:(g + 1) * SPAN],
                        ps[qbank][:, 0:512],
                        stat[:, g * SPAN:(g + 1) * SPAN],
                        ALU.mult),
                        waits=(("pe", F["qmm"][g]),))
                    F["y"][g] = P.mst("dv")
                    bankf[qbank] = ("dv", F["y"][g])
                    gbank = 4 + (g % 2)
                    P.op("ac", lambda h, g=g, gbank=gbank, sl=sl: h.activation(
                        gtb[:, sl * 4096 + g * SPAN:sl * 4096 + (g + 1) * SPAN],
                        ps[gbank][:, 0:512], AF.Silu),
                        waits=(("pe", F["gmm"][g]), ("pe", v_wo_prev)))
                    F["silu"][g] = P.mst("ac")
                    bankf[gbank] = ("ac", F["silu"][g])
                    ws = [("dv", F["y"][g]), ("pe", vsp[0])]
                    if g >= 3:
                        ws.append(("pe", F["s2mm"][g - 3]))
                    elif prevF is not None:
                        ws.append(("pe", prevF["s2mm"][g + 5]))
                    P.op("ac", lambda h, g=g: h.activation(
                        ysq[:, (g % 3) * 512:(g % 3 + 1) * 512],
                        stat[:, g * SPAN:(g + 1) * SPAN], AF.Square),
                        waits=tuple(ws))
                    F["sq"][g] = P.mst("ac")
                    if g >= 2:
                        emit_stats(g - 2)
                    if yn_emit is not None:
                        ln1_step()
                for g in range(6, 8):
                    emit_stats(g)
                vsp[0] = P.cnt["pe"]

            rb_vals = {}

            def emit_rowmath(s2_):
                F2 = front[s2_]
                v_statmm = F2["s2mm"][7]
                P.op("dv", lambda h: h.tensor_scalar_mul(
                    rowb[0:1, 0:512], ps[6][0:1, 0:512], -R1024),
                    waits=(("pe", v_statmm),))
                P.op("dv", lambda h: h.tensor_scalar_mul(
                    rowb[0:1, 512:1024], ps[7][0:1, 0:512], R1024))
                P.op("dv", lambda h: h.tensor_tensor(
                    rowb[0:1, 1024:1536], rowb[0:1, 0:512], rowb[0:1, 0:512],
                    ALU.mult))
                P.op("dv", lambda h: h.tensor_tensor(
                    rowb[0:1, 512:1024], rowb[0:1, 512:1024], rowb[0:1, 1024:1536],
                    ALU.subtract))
                P.op("dv", lambda h: h.tensor_scalar_add(
                    rowb[0:1, 512:1024], rowb[0:1, 512:1024], LN_EPS))
                v_var = P.mst("dv")
                P.op("ac", lambda h: h.activation(
                    rowb[0:1, 1024:1536], rowb[0:1, 512:1024], AF.Sqrt),
                    waits=(("dv", v_var),))
                v_sd = P.mst("ac")
                P.op("dv", lambda h: h.reciprocal(
                    rowb[0:1, 512:1024], rowb[0:1, 1024:1536]),
                    waits=(("ac", v_sd),))
                v_rstd = P.mst("dv")
                P.op("pe", lambda h: h.matmul(
                    ps[6][:, 0:512], onesr[0:1, :], rowb[0:1, 0:512],
                    start=True, stop=True),
                    waits=(("dv", v_var), bankf[6]))
                v_negb = P.mst("pe")
                P.op("pe", lambda h: h.matmul(
                    ps[7][:, 0:512], onesr[0:1, :], rowb[0:1, 512:1024],
                    start=True, stop=True),
                    waits=(("dv", v_rstd), bankf[7]))
                v_rstb = P.mst("pe")
                P.op("dv", lambda h: h.tensor_copy(
                    bcb[:, 0:512], ps[6][:, 0:512]),
                    waits=(("pe", v_negb),))
                P.op("dv", lambda h: h.tensor_copy(
                    bcb[:, 512:1024], ps[7][:, 0:512]),
                    waits=(("pe", v_rstb),))
                v_bcb = P.mst("dv")
                bankf[6] = ("dv", v_bcb)
                bankf[7] = ("dv", v_bcb)
                rb_vals[s2_] = v_bcb

            ln1_gens = []
            v_sqw = [0]

            def emit_ln1(hs, hsl, v_r1):
                v_s2red = 0
                for ttl in range(2):
                    a0 = hsl * 2048 + ttl * 1024
                    P.op("dv", lambda h, a0=a0, ttl=ttl: h.reduce_sum(
                        rcol[:, ttl:ttl + 1], x1st[:, a0:a0 + 1024],
                        AX.X, negate=True),
                        waits=(("po", v_x1_prev[0]),))
                    _v = P.cnt["dv"]
                    P.op("po", lambda h, ttl=ttl: h.tensor_scalar_mul(
                        rcol[:, ttl:ttl + 1], rcol[:, ttl:ttl + 1], R1024),
                        waits=(("dv", _v),))
                    yield
                    P.op("ac", lambda h, a0=a0: h.activation(
                        r1sq[:], x1st[:, a0:a0 + 1024], AF.Square),
                        waits=(("dv", v_r1), ("dv", v_sqw[0])))
                    vsq2 = P.mst("ac")
                    P.op("dv", lambda h, ttl=ttl: h.reduce_sum(
                        rcol[:, 2 + ttl:3 + ttl], r1sq[:], AX.X),
                        waits=(("ac", vsq2),))
                    v_s2red = P.cnt["dv"]
                    v_sqw[0] = v_s2red
                    _v = P.cnt["dv"]
                    P.op("po", lambda h, ttl=ttl: h.tensor_scalar(
                        rcol[:, 2 + ttl:3 + ttl], rcol[:, 2 + ttl:3 + ttl],
                        R1024, None, op0=ALU.mult),
                        waits=(("dv", _v),))
                    P.op("po", lambda h, ttl=ttl: h.tensor_tensor(
                        rcol[:, 6:7], rcol[:, ttl:ttl + 1], rcol[:, ttl:ttl + 1],
                        ALU.mult))
                    P.op("po", lambda h, ttl=ttl: h.tensor_tensor(
                        rcol[:, 4 + ttl:5 + ttl], rcol[:, 2 + ttl:3 + ttl],
                        rcol[:, 6:7], ALU.subtract))
                    if ttl == 0:
                        yield
                P.op("po", lambda h: h.tensor_scalar_add(
                    rcol[:, 4:6], rcol[:, 4:6], LN_EPS))
                v_v = P.mst("po")
                P.op("ac", lambda h: h.activation(
                    rcol[:, 6:8], rcol[:, 4:6], AF.Ln),
                    waits=(("po", v_v),))
                v_sdc = P.mst("ac")
                P.op("po", lambda h: h.tensor_scalar_mul(
                    rcol[:, 6:8], rcol[:, 6:8], -0.5),
                    waits=(("ac", v_sdc),))
                v_hf = P.mst("po")
                P.op("ac", lambda h: h.activation(
                    rcol[:, 6:8], rcol[:, 6:8], AF.Exp),
                    waits=(("po", v_hf),))
                v_sdc = P.mst("ac")
                for ttl in range(2):
                    a0 = hsl * 2048 + ttl * 1024
                    P.op("po", lambda h, a0=a0, ttl=ttl: h.tensor_scalar(
                        x1st[:, a0:a0 + 1024], x1st[:, a0:a0 + 1024],
                        rcol[:, ttl:ttl + 1], rcol[:, 6 + ttl:7 + ttl],
                        op0=ALU.add, op1=ALU.mult),
                        waits=(("ac", v_sdc),))
                v_x1 = P.mst("po")
                v_x1_prev[0] = v_x1
                vz = dma("ac", x1s3[:, 2 * hs:2 * hs + 2, :],
                         x1st[:, hsl * 2048:(hsl + 1) * 2048]
                         .rearrange("p (a d) -> p a d", a=2),
                         "ds%d" % hsl,
                         waits=(("po", v_x1),))
                x1slot_ds[hsl] = vz
                if stage != 2:
                    x1o_do[hsl] = dma(
                        "ac", out3[:, 2 * hs:2 * hs + 2, :],
                        x1st[:, hsl * 2048:(hsl + 1) * 2048]
                        .rearrange("p (a d) -> p a d", a=2),
                        "do%d" % hsl)

            def ln1_step():
                while ln1_gens:
                    try:
                        next(ln1_gens[0])
                        return
                    except StopIteration:
                        ln1_gens.pop(0)

            def ln1_drain():
                while ln1_gens:
                    for _ in ln1_gens.pop(0):
                        pass

            if att_spans:
                emit_front(0)
                emit_rowmath(0)

            x1cast = [0] * NRANGE

            def emit_x1cast(r):
                x1cast[r] = dma(
                    "po", x1b[r * RANGE:(r + 1) * RANGE, :],
                    x1scr[r * RANGE:(r + 1) * RANGE, :],
                    "dxc%d" % r,
                    waits=(("ds0", 16 * (2 * r + 2)), ("ds1", 16 * (2 * r + 2))))

            for s_ in range(att_spans):
                if stage == 2 and s_ in (3, 5, 7):
                    emit_x1cast((s_ - 3) // 2)
                sl = s_ % 2
                F = front[s_]
                v_y = F["y"]
                v_silu = F["silu"]
                v_s2mm = F["s2mm"]
                v_statmm = F["s2mm"][7]

                if s_ + 1 < att_spans:
                    emit_transposes(s_ + 1)

                v_bcb = rb_vals[s_]
                v_y2b = [0] * 8
                v_ynbb = [0] * 8

                def yn_emit(g, F=F, sl=sl, v_bcb=v_bcb, v_y2b=v_y2b,
                            v_ynbb=v_ynbb, v_silu=v_silu):
                    yt = ynts[g % 2]
                    w = [("dv", v_bcb)]
                    if g >= 2:
                        w.append(("po", v_ynbb[g - 2]))
                    P.op("dv", lambda h, g=g, yt=yt: h.tensor_tensor(
                        yt, stat[:, g * SPAN:(g + 1) * SPAN],
                        bcb[:, 0:512], ALU.add),
                        waits=tuple(w))
                    P.op("dv", lambda h, g=g, yt=yt: h.tensor_tensor(
                        yt, yt, bcb[:, 512:1024], ALU.mult))
                    v_mul = P.mst("dv")
                    P.op("po", lambda h, g=g, yt=yt: h.tensor_scalar(
                        ynbb[:], yt,
                        lngc[:, g:g + 1], lnbc[:, g:g + 1],
                        op0=ALU.mult, op1=ALU.add),
                        waits=(("dv", v_mul),))
                    v_ynbb[g] = P.mst("po")
                    P.op("po", lambda h, g=g, sl=sl: h.tensor_tensor(
                        gtb[:, sl * 4096 + g * SPAN:sl * 4096 + (g + 1) * SPAN],
                        ynbb[:],
                        gtb[:, sl * 4096 + g * SPAN:sl * 4096 + (g + 1) * SPAN],
                        ALU.mult),
                        waits=(("ac", v_silu[g]),))
                    v_y2b[g] = P.mst("po")

                if s_ + 1 < att_spans:
                    emit_front(s_ + 1, yn_emit)
                    emit_rowmath(s_ + 1)
                else:
                    for g in range(8):
                        yn_emit(g)
                        ln1_step()

                for half in range(2):
                    hs = 2 * s_ + half
                    hsl = hs % 2
                    for ttl in range(2):
                        tt = half * 2 + ttl
                        gt = 4 * s_ + tt
                        slot = gt % 8
                        for oh in range(2):
                            bank = 2 + oh
                            for dyg in range(8):
                                w = (("po", v_y2b[dyg]),)
                                if dyg == 0:
                                    w = w + (("dwt_o", v_wt["o"]), bankf[bank])
                                P.op("pe", lambda h, tt=tt, oh=oh, dyg=dyg, bank=bank, sl=sl:
                                     h.matmul(
                                         ps[bank][:, 0:512],
                                         gtb[:, sl * 4096 + dyg * SPAN + tt * 128:
                                             sl * 4096 + dyg * SPAN + (tt + 1) * 128],
                                         wT["o"][:, dyg * 1024 + oh * 512:
                                                 dyg * 1024 + (oh + 1) * 512],
                                         start=(dyg == 0), stop=(dyg == 7)),
                                     waits=w)
                            v_wo = P.mst("pe")
                            P.op("dv", lambda h, hsl=hsl, ttl=ttl, oh=oh, slot=slot,
                                 bank=bank: h.tensor_tensor(
                                     x1st[:, hsl * 2048 + ttl * 1024 + oh * 512:
                                          hsl * 2048 + ttl * 1024 + (oh + 1) * 512],
                                     ps[bank][:, 0:512],
                                     xtm[:, slot * 1024 + oh * 512:
                                         slot * 1024 + (oh + 1) * 512],
                                     ALU.add),
                                 waits=(("pe", v_wo),
                                        ("ds%d" % hsl, x1slot_ds[hsl]),
                                        ("do%d" % hsl, x1o_do[hsl])))
                            bankf[bank] = ("dv", P.mst("dv"))
                        slot_free[slot] = ("dv", P.cnt["dv"])
                    ln1_gens.append(emit_ln1(hs, hsl, P.cnt["dv"]))
                v_wo_prev = P.cnt["pe"]
                xf_free[sl] = ("pe", front[s_]["gmm"][7])
                v_yn_prev = P.cnt["dv"]
                for gt in range(4 * s_ + 8, 4 * s_ + 12):
                    if gt < 4 * att_spans and gt not in xld:
                        load_tile(gt)

            pre_drain = {e: P.cnt.get(e, 0) for e in ("pe", "dv", "ac", "po")}
            ln1_drain()

            att_end = {e: P.cnt.get(e, 0) for e in ("pe", "dv", "ac", "po")}
            v_ds_all = P.cnt.get("ds", 0)

            P.mark_cut()
            if stage != 2:
                with nc.Block("attn") as blockA:
                    blockA.tensor(lambda h: P.replay("pe", h, sems, 0))
                    blockA.vector(lambda h: P.replay("dv", h, sems, 0))
                    blockA.scalar(lambda h: P.replay("ac", h, sems, 0))
                    blockA.gpsimd(lambda h: P.replay("po", h, sems, 0))
                    blockA.sync(lambda h: P.replay("sp", h, sems, 0))

        if stage != 2:
            dnames = (["dc", "db1", "dw", "dmeb", "dla", "dlb"]
                      + ["dwc_%s" % n for n in ("q", "v", "g", "o")]
                      + ["dwt_%s" % n for n in ("q", "v", "g", "o")]
                      + ["dx%d" % i for i in range(8)]
                      + ["ds0", "ds1", "do0", "do1"])
            dtot = [(k, P.cnt.get(k, 0)) for k in dnames]
            ctot = [(k, P.cnt.get(k, 0)) for k in ("pe", "dv", "ac")]

            def fin(h):
                for k, v in dtot + ctot:
                    if v > 0:
                        h.wait_ge(sems[k], v)
                if stage in ("setup", "look"):
                    h.dma_start(out=out[0:128, 0:8], in_=sinit[:]).then_inc(sems["do0"], 16)
                    h.wait_ge(sems["do0"], P.cnt.get("do0", 0) + 16)
            with nc.Block("fin") as blockF:
                blockF.sync(fin)
            return nc

        att_dma = [(k, P.cnt.get(k, 0)) for k in
                   (["dc", "db1", "dw", "dla", "dlb", "ds0", "ds1", "do0", "do1"]
                    + ["dwc_%s" % n for n in ("q", "v", "g", "o")]
                    + ["dwt_%s" % n for n in ("q", "v", "g", "o")]
                    + ["dx%d" % i for i in range(8)])
                   if P.cnt.get(k, 0) > 0]
        barrier = tuple([("pe", att_end["pe"]), ("dv", att_end["dv"]),
                         ("ac", att_end["ac"]), ("po", att_end["po"])] + att_dma)
        with ExitStack() as moe:
            x1fmb = sb(moe, "x1fmb", [128, 2 * 8 * 1024], BF16)
            wbuf = [sb(moe, "wb%d" % i, [128, 16384], BF16) for i in range(3)]
            acc = sb(moe, "acc", [128, 8 * 1024], F32)
            hb = sb(moe, "hb", [128, 16 * 512], BF16)
            sqm = sb(moe, "sqm", [128, 512], F32)
            o2st = sb(moe, "o2st", [128, 2048], F32)
            gts = sb(moe, "gts", [128, 8], F32)
            tt1 = sb(moe, "tt1", [128, 8], F32)
            tt2 = sb(moe, "tt2", [128, 8], F32)
            cwsb = sb(moe, "cwsb", [128, 2 * 64], F32)
            lgs = sb(moe, "lgs", [128, 64], F32)
            cwF = sb(moe, "cwF", [NE, 1024], BF16)

            order = []
            for r in range(NRANGE):
                es = list(range(NE)) if r % 2 == 0 else list(range(NE - 1, -1, -1))
                for e in es:
                    order.append((r, e))
            NK = len(order)
            bufW1 = [0] * NK
            bufW2 = [0] * NK
            newload = [False] * NK
            cur1, cur2, nxt = 0, 1, 2
            for k in range(NK):
                if k == 0:
                    newload[0] = True
                    bufW1[0], bufW2[0] = cur1, cur2
                elif order[k][1] == order[k - 1][1]:
                    bufW1[k], bufW2[k] = bufW1[k - 1], bufW2[k - 1]
                    cur1, cur2 = bufW1[k], bufW2[k]
                else:
                    newload[k] = True
                    o1, o2 = cur1, cur2
                    cur1 = nxt
                    cur2 = o1
                    nxt = o2
                    bufW1[k], bufW2[k] = cur1, cur2

            w13 = w1.rearrange("e (kg p) j -> e p kg j", p=128)
            w23 = w2.rearrange("e (jg p) o -> e p jg o", p=128)

            def load_w1(e, buf, waits):
                v = dma("po", wbuf[buf][:].rearrange("p (kg j) -> p kg j", kg=8),
                        w13[e], "dwb%d" % buf, waits=waits)
                return ("dwb%d" % buf, v)

            def load_w2(e, buf, waits):
                v = dma("po", wbuf[buf][:].rearrange("p (jg o) -> p jg o", jg=16),
                        w23[e], "dwb%d" % buf, waits=waits)
                return ("dwb%d" % buf, v)

            v_w1ld = [None] * NK
            v_w2ld = [None] * NK
            nl = [k for k in range(1, NK) if newload[k]]
            w1_after = {}
            for j in range(1, len(nl)):
                kj = nl[j]
                tb = bufW1[kj]
                kprev = max(k2 for k2 in range(kj) if bufW2[k2] == tb)
                w1_after.setdefault(kprev, []).append(kj)

            v_xT = [0] * NRANGE
            v_cwt = [0] * 8
            v_hdone = 0
            v_edone = 0
            v_hbfree = 0
            v_accinit = [0] * NRANGE
            v_ln2 = 0
            v_moe_prev = 0
            v_fmb_user = {0: 0, 1: 0}
            v_cwall = [0] * NRANGE
            v_biasmm = [0]
            o2_do = {0: 0, 1: 0}

            def emit_head_a(r):
                sl = r % 2
                fmb = x1fmb[:, sl * 8192:(sl + 1) * 8192]
                c_sem = "dxc%d" % r
                t0s, t1s = "drb%d" % (2 * (r % 2)), "drb%d" % (2 * (r % 2) + 1)
                for kg in range(8):
                    wl = ()
                    if kg == 0:
                        wl = ((c_sem, x1cast[r]), ("pe", v_fmb_user[sl]))
                        if r == 0:
                            wl = wl + (("pe", front[NSPAN - 1]["vmm"][7]),)
                    dma("sp", fmb[:, kg * 1024:(kg + 1) * 1024],
                        x1b[r * RANGE:(r + 1) * RANGE, kg * 128:(kg + 1) * 128],
                        t0s if kg % 2 == 0 else t1s, waits=wl, transpose=True)
                v_xT[r] = ((t0s, P.cnt[t0s]), (t1s, P.cnt[t1s]))

            def emit_head_b(r):
                sl = r % 2
                fmb = x1fmb[:, sl * 8192:(sl + 1) * 8192]
                for tt in range(8):
                    bank = 6 + (tt % 2)
                    for kg in range(8):
                        w = ()
                        if kg == 0:
                            w = v_xT[r] + (("dmeb", v_meb), bankf[bank])
                        P.op("pe", lambda h, tt=tt, kg=kg, bank=bank, fmb=fmb: h.matmul(
                            ps[bank][:, 0:8],
                            fmb[:, kg * 1024 + tt * 128:kg * 1024 + (tt + 1) * 128],
                            gwb[:, kg * 8:(kg + 1) * 8],
                            start=(kg == 0), stop=(kg == 7)),
                            waits=w)
                    v_lg = P.mst("pe")
                    P.op("dv", lambda h, tt=tt, bank=bank: h.tensor_copy(
                        lgs[:, tt * 8:(tt + 1) * 8], ps[bank][:, 0:8]),
                        waits=(("pe", v_lg),))
                    bankf[bank] = ("dv", P.mst("dv"))
                for tt in range(8):
                    P.op("dv", lambda h, tt=tt: h.reduce_max(
                        rcol[:, 0:1], lgs[:, tt * 8:(tt + 1) * 8], AX.X,
                        negate=True))
                    v_nm = P.mst("dv")
                    P.op("ac", lambda h, tt=tt: h.activation(
                        gts[:], lgs[:, tt * 8:(tt + 1) * 8], AF.Exp,
                        bias=rcol[:, 0:1]),
                        waits=(("dv", v_nm),))
                    v_exp = P.mst("ac")
                    P.op("dv", lambda h: h.reduce_sum(rcol[:, 1:2], gts[:], AX.X),
                         waits=(("ac", v_exp),))
                    P.op("dv", lambda h: h.reciprocal(rcol[:, 2:3], rcol[:, 1:2]))
                    P.op("dv", lambda h: h.tensor_scalar_mul(
                        gts[:], gts[:], rcol[:, 2:3]))
                    P.op("dv", lambda h: h.reduce_max(rcol[:, 3:4], gts[:], AX.X))
                    P.op("dv", lambda h: h.tensor_scalar(
                        tt1[:], gts[:], rcol[:, 3:4], None, op0=ALU.is_ge))
                    P.op("dv", lambda h: h.scalar_tensor_tensor(
                        tt2[:], tt1[:], -BIG, gts[:], op0=ALU.mult, op1=ALU.add))
                    P.op("dv", lambda h: h.reduce_max(rcol[:, 4:5], tt2[:], AX.X))
                    P.op("dv", lambda h: h.tensor_scalar(
                        tt1[:], gts[:], rcol[:, 4:5], None, op0=ALU.is_ge))
                    P.op("dv", lambda h: h.tensor_tensor(
                        rcol[:, 5:6], rcol[:, 3:4], rcol[:, 4:5], ALU.add))
                    P.op("dv", lambda h: h.tensor_scalar_add(
                        rcol[:, 5:6], rcol[:, 5:6], 1e-8))
                    P.op("dv", lambda h: h.reciprocal(rcol[:, 6:7], rcol[:, 5:6]))
                    P.op("dv", lambda h: h.tensor_tensor(
                        gts[:], gts[:], tt1[:], ALU.mult))
                    P.op("dv", lambda h, tt=tt, sl=sl: h.tensor_scalar_mul(
                        cwsb[:, sl * 64 + tt * 8:sl * 64 + (tt + 1) * 8],
                        gts[:], rcol[:, 6:7]))
                    v_cwt[tt] = P.mst("dv")

            def emit_head_c(r):
                sl = r % 2
                for tt in range(8):
                    bank = 6 + (tt % 2)
                    P.op("pe", lambda h, tt=tt, bank=bank, sl=sl: h.transpose(
                        ps[bank][0:8, 0:128],
                        cwsb[:, sl * 64 + tt * 8:sl * 64 + (tt + 1) * 8], idfs[:]),
                        waits=(("dv", v_cwt[tt]), bankf[bank]))
                    v_p = P.mst("pe")
                    P.op("dv", lambda h, tt=tt, bank=bank: h.tensor_copy(
                        cwF[0:NE, tt * 128:(tt + 1) * 128],
                        ps[bank][0:NE, 0:128]),
                        waits=(("pe", v_p), ("pe", v_biasmm[0])))
                    bankf[bank] = ("dv", P.mst("dv"))
                v_cwall[r] = P.cnt["dv"]

            def emit_ln2_stats(tt):
                a0 = tt * 1024
                P.op("dv", lambda h, a0=a0, tt=tt: h.reduce_sum(
                    rcol[:, tt:tt + 1], acc[:, a0:a0 + 1024], AX.X,
                    negate=True))
                P.op("dv", lambda h, tt=tt: h.tensor_scalar_mul(
                    rcol[:, tt:tt + 1], rcol[:, tt:tt + 1], R1024))
                P.op("dv", lambda h, a0=a0: h.tensor_tensor(
                    sqm[:], acc[:, a0:a0 + 512], acc[:, a0:a0 + 512],
                    ALU.mult))
                P.op("dv", lambda h: h.reduce_sum(gts[:, 0:1], sqm[:], AX.X))
                P.op("dv", lambda h, a0=a0: h.tensor_tensor(
                    sqm[:], acc[:, a0 + 512:a0 + 1024],
                    acc[:, a0 + 512:a0 + 1024], ALU.mult))
                P.op("dv", lambda h: h.reduce_sum(gts[:, 1:2], sqm[:], AX.X))
                P.op("dv", lambda h: h.tensor_tensor(
                    gts[:, 0:1], gts[:, 0:1], gts[:, 1:2], ALU.add))
                P.op("dv", lambda h: h.tensor_scalar_mul(
                    gts[:, 0:1], gts[:, 0:1], R1024))
                P.op("dv", lambda h, tt=tt: h.tensor_tensor(
                    gts[:, 1:2], rcol[:, tt:tt + 1], rcol[:, tt:tt + 1],
                    ALU.mult))
                P.op("dv", lambda h, tt=tt: h.tensor_tensor(
                    rcol2[:, tt:tt + 1], gts[:, 0:1], gts[:, 1:2],
                    ALU.subtract))

            dsf = tuple((k, P.cnt[k]) for k in ("ds0", "ds1") if P.cnt.get(k, 0))
            v_w1ld[0] = load_w1(order[0][1], bufW1[0],
                                (("pe", att_end["pe"]),))
            v_w2ld[0] = load_w2(order[0][1], bufW2[0],
                                (("dv", pre_drain["dv"]),))
            emit_x1cast(NRANGE - 1)
            emit_head_a(0)
            P.op("dv", None, waits=(("po", att_end["po"]),) + dsf)
            P.op("ac", None,
                 waits=(("po", att_end["po"]), ("dv", att_end["dv"])) + dsf)
            P.op("po", None,
                 waits=(("pe", att_end["pe"]), ("dv", att_end["dv"])))
            if nl:
                v_w1ld[nl[0]] = load_w1(order[nl[0]][1], bufW1[nl[0]], ())
            emit_head_b(0)
            emit_head_c(0)

            for r in range(NRANGE):
                sl = r % 2
                fmb = x1fmb[:, sl * 8192:(sl + 1) * 8192]
                aw = (("ds0", 16 * (2 * r + 2)), ("ds1", 16 * (2 * r + 2)),
                      ("dv", v_ln2))
                if r == 0:
                    aw = aw + barrier
                v_accinit[r] = dma(
                    "sp", acc[:].rearrange("p (a d) -> p a d", a=8),
                    x1s3[:, 8 * r:8 * r + 8, :], "dr",
                    waits=aw)
                for tt8 in range(8):
                    P.op("dv", lambda h, tt8=tt8: h.tensor_scalar_mul(
                        acc[:, tt8 * 1024:(tt8 + 1) * 1024],
                        acc[:, tt8 * 1024:(tt8 + 1) * 1024], 2.0),
                        waits=(("dr", v_accinit[r]),))
                v_acc2 = P.mst("dv")

                base_k = 8 * r
                for ki in range(NE):
                    k = base_k + ki
                    (rr, e) = order[k]
                    for sp2 in range(2):
                        v_h = [0] * 16
                        v_relu = [0] * 16
                        for jg in range(16):
                            bank = jg % 2
                            for kg in range(8):
                                w = ()
                                if kg == 0:
                                    w = (v_w1ld[k], bankf[bank]) + v_xT[r]
                                P.op("pe", lambda h, jg=jg, kg=kg, sp2=sp2, bank=bank,
                                     wb=wbuf[bufW1[k]], fmb=fmb: h.matmul(
                                         ps[bank][:, 0:512],
                                         wb[:, kg * 2048 + jg * 128:kg * 2048 + (jg + 1) * 128],
                                         fmb[:, kg * 1024 + sp2 * 512:kg * 1024 + (sp2 + 1) * 512],
                                         start=(kg == 0), stop=(kg == 7)),
                                     waits=w)
                            v_h[jg] = P.mst("pe")
                            P.op("ac", lambda h, jg=jg, e=e, bank=bank: h.activation(
                                hb[:, jg * 512:(jg + 1) * 512], ps[bank][:, 0:512],
                                AF.Relu, bias=b1T[:, jg * 8 + e:jg * 8 + e + 1]),
                                waits=(("pe", v_h[jg]), ("pe", v_hbfree)))
                            v_relu[jg] = P.mst("ac")
                            bankf[bank] = ("ac", v_relu[jg])
                        v_hdone = P.cnt["pe"]
                        if sp2 == 1 and k + 1 < NK and newload[k + 1]:
                            v_w2ld[k + 1] = load_w2(order[k + 1][1], bufW2[k + 1],
                                                    (("pe", v_hdone),))
                        for tl in range(4):
                            for oh in range(2):
                                bank = 2 + (tl * 2 + oh) % 4
                                gt = sp2 * 4 + tl
                                for jg in range(16):
                                    w2w = []
                                    if jg == 0:
                                        w2w = [v_w2ld[k], bankf[bank],
                                               ("ac", v_relu[3])]
                                    elif jg in (4, 8, 12):
                                        w2w = [("ac", v_relu[min(jg + 3, 15)])]
                                    P.op("pe", lambda h, jg=jg, tl=tl, oh=oh, bank=bank,
                                         wb=wbuf[bufW2[k]]: h.matmul(
                                             ps[bank][:, 0:512],
                                             hb[:, jg * 512 + tl * 128:jg * 512 + (tl + 1) * 128],
                                             wb[:, jg * 1024 + oh * 512:jg * 1024 + (oh + 1) * 512],
                                             start=(jg == 0), stop=(jg == 15)),
                                         waits=tuple(w2w))
                                v_w2g = P.mst("pe")
                                P.op("dv", lambda h, gt=gt, oh=oh, e=e, bank=bank, sl=sl:
                                     h.scalar_tensor_tensor(
                                         acc[:, gt * 1024 + oh * 512:gt * 1024 + (oh + 1) * 512],
                                         ps[bank][:, 0:512],
                                         cwsb[:, sl * 64 + gt * 8 + e:sl * 64 + gt * 8 + e + 1],
                                         acc[:, gt * 1024 + oh * 512:gt * 1024 + (oh + 1) * 512],
                                         op0=ALU.mult, op1=ALU.add),
                                     waits=(("pe", v_w2g),))
                                bankf[bank] = ("dv", P.mst("dv"))
                                if ki == NE - 1 and oh == 1:
                                    emit_ln2_stats(gt)
                        v_hbfree = P.cnt["pe"]
                    v_edone = P.cnt["pe"]
                    v_moe_prev = v_edone
                    for kj in w1_after.get(k, ()):
                        v_w1ld[kj] = load_w1(order[kj][1], bufW1[kj],
                                             (("pe", v_edone),))
                    if k + 1 < NK and not newload[k + 1]:
                        v_w1ld[k + 1] = v_w1ld[k]
                        v_w2ld[k + 1] = v_w2ld[k]
                    if ki == 0:
                        for tt in range(8):
                            for oh in range(2):
                                bank = 6 + ((2 * tt + oh) % 2)
                                P.op("pe", lambda h, tt=tt, oh=oh, bank=bank:
                                     h.matmul(
                                         ps[bank][:, 0:512],
                                         cwF[0:NE, tt * 128:(tt + 1) * 128],
                                         b2b[0:NE, oh * 512:(oh + 1) * 512],
                                         start=True, stop=True),
                                     waits=(("dmeb", v_meb), bankf[bank]))
                                v_p = P.mst("pe")
                                P.op("dv", lambda h, tt=tt, oh=oh, bank=bank:
                                     h.tensor_tensor(
                                         acc[:, tt * 1024 + oh * 512:tt * 1024 + (oh + 1) * 512],
                                         ps[bank][:, 0:512],
                                         acc[:, tt * 1024 + oh * 512:tt * 1024 + (oh + 1) * 512],
                                         ALU.add),
                                     waits=(("pe", v_p),))
                                bankf[bank] = ("dv", P.mst("dv"))
                        v_biasmm[0] = P.cnt["pe"]
                    if r + 1 < NRANGE:
                        if ki == 1:
                            emit_head_a(r + 1)
                        elif ki == 2:
                            emit_head_b(r + 1)
                        elif ki == 3:
                            emit_head_c(r + 1)
                v_fmb_user[sl] = v_edone

                P.op("dv", lambda h: h.tensor_scalar_add(
                    rcol2[:], rcol2[:], LN_EPS))
                v_v = P.mst("dv")
                P.op("ac", lambda h: h.activation(rcol2[:], rcol2[:], AF.Sqrt),
                     waits=(("dv", v_v),))
                v_sdc = P.mst("ac")
                P.op("dv", lambda h: h.reciprocal(rcol2[:], rcol2[:]),
                     waits=(("ac", v_sdc),))
                for tt in range(8):
                    a0 = tt * 1024
                    o2c = (tt % 2) * 1024
                    P.op("dv", lambda h, o2c=o2c, a0=a0, tt=tt: h.tensor_scalar(
                        o2st[:, o2c:o2c + 1024],
                        acc[:, a0:a0 + 1024], rcol[:, tt:tt + 1],
                        rcol2[:, tt:tt + 1],
                        op0=ALU.add, op1=ALU.mult),
                        waits=(("do0", o2_do[0]),))
                    v_o2 = P.mst("dv")
                    if tt % 2 == 1:
                        vdo = dma("sp", out3[:, 8 * r + tt - 1:8 * r + tt + 1, :],
                                  o2st[:].rearrange("p (a d) -> p a d", a=2),
                                  "do0",
                                  waits=(("dv", v_o2),))
                        o2_do[0] = vdo
                v_ln2 = P.cnt["dv"]

            P.op("sp", None, waits=(("do0", P.cnt.get("do0", 0)),
                                    ("do1", P.cnt.get("do1", 0))))


            with nc.Block("moe") as blockB:
                blockB.tensor(lambda h: P.replay("pe", h, sems, None))
                blockB.vector(lambda h: P.replay("dv", h, sems, None))
                blockB.scalar(lambda h: P.replay("ac", h, sems, None))
                blockB.gpsimd(lambda h: P.replay("po", h, sems, None))
                blockB.sync(lambda h: P.replay("sp", h, sems, None))

    return nc


_CACHED = {}


def _get_program(stage):
    if stage not in _CACHED:
        _CACHED[stage] = build_program(stage)
    return _CACHED[stage]


def kernel(L0_attn_x, L0_attn_sin, L0_attn_cos, L0_attn_decay, L0_attn_Wq,
           L0_attn_Wv, L0_attn_Wo, L0_attn_Wg, L0_attn_ln_g, L0_attn_ln_b,
           L0_ffn_gate_W, L0_ffn_W1, L0_ffn_b1, L0_ffn_W2, L0_ffn_b2):
    x = np.ascontiguousarray(L0_attn_x, dtype=np.float32)
    idfv = np.eye(128, dtype=np.float32)
    idbv = np.eye(128).astype(ml_dtypes.bfloat16)
    erv = np.broadcast_to(np.arange(LCH - 1, -1, -1, dtype=np.float32),
                          (NH, LCH)).copy()

    nc = _get_program(STAGE)

    shared = dict(
        wq=np.ascontiguousarray(L0_attn_Wq, dtype=np.float32),
        wv=np.ascontiguousarray(L0_attn_Wv, dtype=np.float32),
        wg=np.ascontiguousarray(L0_attn_Wg, dtype=np.float32),
        wo=np.ascontiguousarray(L0_attn_Wo, dtype=np.float32),
        dec=np.ascontiguousarray(L0_attn_decay, dtype=np.float32),
        lng=np.ascontiguousarray(L0_attn_ln_g, dtype=np.float32),
        lnb=np.ascontiguousarray(L0_attn_ln_b, dtype=np.float32),
        gwt=np.ascontiguousarray(L0_ffn_gate_W, dtype=np.float32),
        w1=np.ascontiguousarray(L0_ffn_W1, dtype=np.float32),
        b1=np.ascontiguousarray(L0_ffn_b1, dtype=np.float32),
        w2=np.ascontiguousarray(L0_ffn_W2, dtype=np.float32),
        b2=np.ascontiguousarray(L0_ffn_b2, dtype=np.float32),
        idf=idfv, idb=idbv, erow=erv,
    )
    in_maps = []
    for c in range(8):
        b, qt = divmod(c, 4)
        xc = np.ascontiguousarray(x[b, qt * TPC:(qt + 1) * TPC])
        xl = np.zeros((LOOK, D), dtype=np.float32)
        if qt > 0:
            n = qt * TPC
            xl[LOOK - n:] = x[b, 0:n]
        m = dict(shared)
        m["xin"] = xc
        m["xlk"] = xl
        in_maps.append(m)

    res = run_bass_kernel_spmd(nc, in_maps, core_ids=list(range(8)))
    outv = np.empty((B, L, D), dtype=np.float32)
    for c in range(8):
        b, qt = divmod(c, 4)
        outv[b, qt * TPC:(qt + 1) * TPC] = res.results[c]["out"]
    return outv


# version 26
# speedup vs baseline: 1.1298x; 1.1298x over previous
import numpy as np
import ml_dtypes
import concourse.bass as bass
import concourse.mybir as mybir
from concourse.bass_utils import run_bass_kernel_spmd
from contextlib import ExitStack

F32 = mybir.dt.float32
BF16 = mybir.dt.bfloat16
AF = mybir.ActivationFunctionType
ALU = mybir.AluOpType
AX = mybir.AxisListType

B, L, D = 2, 16384, 1024
TPC = 4096
NH = 16
NE, DHID = 8, 2048
SPAN = 512
NSPAN = TPC // SPAN
RANGE = 1024
NRANGE = TPC // RANGE
LOOK = 12288
LCH = 128
LTILE = 4
NLT = LOOK // (LCH * LTILE)
NCH = LOOK // LCH
LN_EPS = 1e-5
BIG = 1e30
R1024 = 1.0 / 1024.0

ENGINES = ("pe", "dv", "ac", "po", "sp")
STAGE = 2


class Plan:
    def __init__(self):
        self.q = {e: [] for e in ENGINES}
        self.cnt = {}
        self.producer = {}
        self.waited = {e: {} for e in ENGINES}
        self.cut = None

    AUTOSYNC = ("dv", "ac", "po")

    def op(self, eng, fn, waits=(), incs=(), is_dma=False):
        auto = (fn is not None) and (not is_dma) and eng in self.AUTOSYNC
        if auto:
            waits = tuple(waits) + ((eng, self.cnt.get(eng, 0)),)
            incs = tuple(incs) + ((eng, 1),)
        fw = []
        for (s, v) in waits:
            if v is None or v <= 0:
                continue
            if self.waited[eng].get(s, 0) >= v:
                continue
            fw.append((s, v))
            self.waited[eng][s] = v
        for (s, amt) in incs:
            self.cnt[s] = self.cnt.get(s, 0) + amt
            p = self.producer.get(s)
            if p is None:
                self.producer[s] = eng
            elif p != eng:
                self.producer[s] = "-"
        self.q[eng].append((tuple(fw), fn, tuple(incs)))

    def mst(self, eng, waits=()):
        assert not waits
        if eng in self.AUTOSYNC:
            return self.cnt[eng]
        q = self.q[eng]
        assert q and q[-1][1] is not None, "mst needs a preceding real op"
        w, fn, incs = q[-1]
        self.cnt[eng] = self.cnt.get(eng, 0) + 1
        q[-1] = (w, fn, incs + ((eng, 1),))
        return self.cnt[eng]

    def mark_cut(self):
        self.cut = {e: len(self.q[e]) for e in ENGINES}

    def replay(self, eng, handle, sems, phase):
        lo = self.cut[eng] if phase == 1 else 0
        hi = self.cut[eng] if phase == 0 else len(self.q[eng])
        for (waits, fn, incs) in self.q[eng][lo:hi]:
            for (s, v) in waits:
                handle.wait_ge(sems[s], v)
            inst = fn(handle) if fn is not None else None
            if incs:
                if inst is None:
                    inst = handle.nop()
                for (s, a) in incs:
                    inst = inst.then_inc(sems[s], a)


def build_program(stage=2):
    att_spans = NSPAN
    do_look = True
    do_moe = (stage == 2)
    if stage == "setup":
        att_spans, do_look = 0, False
    elif stage == "look":
        att_spans = 0
    elif isinstance(stage, tuple) and stage[0] == "spans":
        att_spans = stage[1]
    nc = bass.Bass("TRN2", target_bir_lowering=False)

    xin = nc.dram_tensor("xin", [TPC, D], F32, kind="ExternalInput").ap()
    xlk = nc.dram_tensor("xlk", [LOOK, D], F32, kind="ExternalInput").ap()
    wq = nc.dram_tensor("wq", [D, D], F32, kind="ExternalInput").ap()
    wv = nc.dram_tensor("wv", [D, D], F32, kind="ExternalInput").ap()
    wg = nc.dram_tensor("wg", [D, D], F32, kind="ExternalInput").ap()
    wo = nc.dram_tensor("wo", [D, D], F32, kind="ExternalInput").ap()
    dec = nc.dram_tensor("dec", [NH, 2, 1], F32, kind="ExternalInput").ap()
    lng = nc.dram_tensor("lng", [D], F32, kind="ExternalInput").ap()
    lnb = nc.dram_tensor("lnb", [D], F32, kind="ExternalInput").ap()
    gwt = nc.dram_tensor("gwt", [D, NE], F32, kind="ExternalInput").ap()
    w1 = nc.dram_tensor("w1", [NE, D, DHID], F32, kind="ExternalInput").ap()
    b1 = nc.dram_tensor("b1", [NE, DHID], F32, kind="ExternalInput").ap()
    w2 = nc.dram_tensor("w2", [NE, DHID, D], F32, kind="ExternalInput").ap()
    b2 = nc.dram_tensor("b2", [NE, D], F32, kind="ExternalInput").ap()
    idf = nc.dram_tensor("idf", [128, 128], F32, kind="ExternalInput").ap()
    idb = nc.dram_tensor("idb", [128, 128], BF16, kind="ExternalInput").ap()
    erow = nc.dram_tensor("erow", [NH, LCH], F32, kind="ExternalInput").ap()
    out = nc.dram_tensor("out", [TPC, D], F32, kind="ExternalOutput").ap()

    x1scr = nc.dram_tensor("x1scr", [TPC, D], F32, kind="Internal").ap()
    x1b = nc.dram_tensor("x1b", [TPC, D], BF16, kind="Internal").ap()
    wtscr = {nm: nc.dram_tensor("wtscr_%s" % nm, [D, D], BF16,
                                 kind="Internal").ap()
             for nm in ("q", "v", "g", "o")}

    xin3 = xin.rearrange("(a p) d -> p a d", p=128)
    out3 = out.rearrange("(a p) d -> p a d", p=128)
    x1s3 = x1scr.rearrange("(a p) d -> p a d", p=128)
    xlk3 = xlk.rearrange("(c p) d -> p c d", p=128)

    sems = {}
    P = Plan()

    with ExitStack() as outer:
        semnames = (["dc", "db1", "dw", "dmeb", "dr"]
                    + ["dwc_%s" % n for n in ("q", "v", "g", "o")]
                    + ["dxc%d" % i for i in range(4)]
                    + ["dwt_%s" % n for n in ("q", "v", "g", "o")]
                    + ["dx%d" % i for i in range(8)]
                    + ["dla", "dlb", "ds0", "ds1", "do0", "do1"]
                    + ["dwb%d" % i for i in range(3)]
                    + ["drb%d" % i for i in range(4)]
                    + ["pe", "dv", "ac", "po", "sp"])
        for s in semnames:
            sems[s] = outer.enter_context(nc.semaphore(s))

        def sb(stack, name, shape, dt):
            return stack.enter_context(nc.sbuf_tensor(name, shape, dt))

        idfs = sb(outer, "idfs", [128, 128], F32)
        idbs = sb(outer, "idbs", [128, 128], BF16)
        decs = sb(outer, "decs", [NH, 2], F32)
        lamc = sb(outer, "lamc", [NH, 1], F32)
        llc = sb(outer, "llc", [NH, 1], F32)
        lamr = sb(outer, "lamr", [1, NH], F32)
        lamD = sb(outer, "lamD", [128, 8], F32)
        sinit = sb(outer, "sinit", [128, 8], F32)
        scur = sb(outer, "scur", [128, 8], F32)
        lngc = sb(outer, "lngc", [128, 8], F32)
        lnbc = sb(outer, "lnbc", [128, 8], F32)
        ln8 = sb(outer, "ln8", [8, 256], F32)
        onesc = sb(outer, "onesc", [128, 1], F32)
        onesr = sb(outer, "onesr", [1, 128], F32)
        erows = sb(outer, "erows", [NH, LCH], F32)
        b1s2 = sb(outer, "b1s2", [NE, 512], F32)
        b1T = sb(outer, "b1T", [128, 128], F32)
        b2b = sb(outer, "b2b", [NE, D], BF16)
        gwb = sb(outer, "gwb", [128, 64], BF16)
        rowb = sb(outer, "rowb", [1, 3 * SPAN], F32)
        rcol = sb(outer, "rcol", [128, 8], F32)
        rcol2 = sb(outer, "rcol2", [128, 8], F32)

        ps = [outer.enter_context(nc.psum_tensor("ps%d" % i, [128, 512], F32))
              for i in range(8)]
        bankf = {i: ("dv", 0) for i in range(8)}

        def dma(eng, dst, src, s, waits=(), transpose=False):
            if transpose:
                fn = lambda h, d=dst, r=src: h.dma_start(out=d, in_=r, transpose=True)
            else:
                fn = lambda h, d=dst, r=src: h.dma_start(out=d, in_=r)
            P.op(eng, fn, waits=waits, incs=((s, 16),), is_dma=True)
            return P.cnt[s]

        dma("ac", idfs[:], idf, "dc")
        dma("ac", idbs[:], idb, "dc")
        dma("ac", decs[:], dec.rearrange("h a b -> h (a b)"), "dc")
        dma("ac", ln8[:, 0:128], lng.rearrange("(a b) -> a b", b=128), "dc")
        dma("ac", ln8[:, 128:256], lnb.rearrange("(a b) -> a b", b=128), "dc")
        v_dc = dma("ac", erows[:], erow, "dc")
        v_idf = v_dec = v_ln = v_er = v_dc
        dma("po", b2b[:], b2, "dmeb")
        v_meb = dma("po", gwb[:].rearrange("p (kg e) -> p kg e", kg=8),
                    gwt.rearrange("(kg p) e -> p kg e", p=128), "dmeb")
        v_b2 = v_gw = v_meb

        P.op("dv", lambda h: h.memset(onesc[:], 1.0))
        P.op("dv", lambda h: h.memset(onesr[:], 1.0))
        P.op("dv", lambda h: h.tensor_scalar_max(lamc[:], decs[:, 1:2], 1e-30),
             waits=(("dc", v_dec),))
        v_lamc = P.mst("dv")
        P.op("ac", lambda h: h.activation(llc[:], lamc[:], AF.Ln),
             waits=(("dv", v_lamc),))
        v_llc = P.mst("ac")
        P.op("pe", lambda h: h.transpose(ps[6][0:1, 0:NH], lamc[:], idfs[0:NH, 0:NH]),
             waits=(("dv", v_lamc), ("dc", v_idf)))
        v_p = P.mst("pe")
        P.op("dv", lambda h: h.tensor_copy(lamr[:], ps[6][0:1, 0:NH]),
             waits=(("pe", v_p),))
        v_lamr = P.mst("dv")
        P.op("pe", lambda h: h.matmul(ps[6][:, 0:NH], onesr[0:1, :], lamr[0:1, :],
                                      start=True, stop=True),
             waits=(("dv", v_lamr),))
        v_p = P.mst("pe")
        for g in range(8):
            P.op("dv", lambda h, g=g: h.tensor_copy(
                lamD[0:64, g:g + 1], ps[6][0:64, 2 * g:2 * g + 1]),
                waits=(("pe", v_p),))
            P.op("dv", lambda h, g=g: h.tensor_copy(
                lamD[64:128, g:g + 1], ps[6][64:128, 2 * g + 1:2 * g + 2]))
        v_lamD = P.mst("dv")
        bankf[6] = ("dv", v_lamD)
        P.op("pe", lambda h: h.transpose(ps[7][:, 0:8], ln8[:, 0:128],
                                         idfs[0:8, 0:8]), waits=(("dc", v_ln),))
        P.op("pe", lambda h: h.transpose(ps[7][:, 8:16], ln8[:, 128:256],
                                         idfs[0:8, 0:8]))
        v_p = P.mst("pe")
        P.op("dv", lambda h: h.tensor_copy(lngc[:], ps[7][:, 0:8]),
             waits=(("pe", v_p),))
        P.op("dv", lambda h: h.tensor_copy(lnbc[:], ps[7][:, 8:16]))
        P.mst("dv")
        for bc in range(4):
            v_b1 = dma("ac", b1s2[:], b1[:, bc * 512:(bc + 1) * 512], "db1",
                       waits=(("pe", P.cnt["pe"]),))
            for j in range(4):
                jg = bc * 4 + j
                P.op("pe", lambda h, j=j: h.transpose(
                    ps[7][:, 0:8], b1s2[0:NE, j * 128:(j + 1) * 128],
                    idfs[0:NE, 0:NE]),
                    waits=(("db1", v_b1), ("dv", P.cnt["dv"])))
                v_p = P.mst("pe")
                P.op("dv", lambda h, jg=jg: h.tensor_copy(
                    b1T[:, jg * 8:(jg + 1) * 8], ps[7][:, 0:8]),
                    waits=(("pe", v_p),))
                P.mst("dv")
        bankf[7] = ("dv", P.cnt["dv"])

        with ExitStack() as att:
            wT = {nm: sb(att, "w%sT" % nm, [128, 8 * 1024], BF16)
                  for nm in ("q", "v", "g", "o")}
            xtm = sb(att, "xtm", [128, 8 * 1024], F32)
            xfmb = sb(att, "xfmb", [128, 2 * 4096], BF16)
            xlt = sb(att, "xlt", [128, 2 * 4096], BF16)
            dcT = sb(att, "dcT", [NH, 4 * LCH], F32)
            ofc = sb(att, "ofc", [NH, 8], F32)
            dcbu = sb(att, "dcbu", [128, 2 * 64], BF16)
            xbarb = sb(att, "xbarb", [128, 128], BF16)
            stat = sb(att, "stat", [128, 8 * SPAN], F32)
            ysq = sb(att, "ysq", [128, 3 * 512], F32)
            ynt = sb(att, "ynt", [128, 512], F32)
            ynbb = sb(att, "ynbb", [128, 512], BF16)
            vsb = sb(att, "vsb", [128, 2 * 512], BF16)
            gtb = sb(att, "gtb", [128, 2 * 8 * SPAN], BF16)
            r1sq = sb(att, "r1sq", [128, 1024], F32)
            x1st = sb(att, "x1st", [128, 2 * 2048], F32)
            bcb = xlt[:, 0:2048].bitcast(F32)
            ynts = [ynt[:, 0:512], xlt[:, 2048:3072].bitcast(F32)]

            v_wt = {}

            wcv = {}

            def cast_weight_dram(nm, src):
                wcv[nm] = dma("po", wtscr[nm], src, "dwc_%s" % nm)

            def cast_weight(nm):
                for kg in range(8):
                    dma("sp", wT[nm][:, kg * 1024:(kg + 1) * 1024],
                        wtscr[nm][:, kg * 128:(kg + 1) * 128], "dwt_%s" % nm,
                        waits=(("dwc_%s" % nm, wcv[nm]),) if kg == 0 else (),
                        transpose=True)
                v_wt[nm] = P.cnt["dwt_%s" % nm]

            cast_weight_dram("v", wv)
            cast_weight_dram("q", wq)
            cast_weight_dram("g", wg)
            cast_weight_dram("o", wo)
            cast_weight("v")

            lk_done = []
            for t in range(NLT if do_look else 0):
                par = t % 2
                buf = xlt[:, par * 4096:(par + 1) * 4096]
                w = (("pe", lk_done[t - 2]),) if t >= 2 else ()
                eng, dsem = ("po", "dla") if par == 0 else ("po", "dlb")
                vld = dma(eng, buf.rearrange("p (c d) -> p c d", c=LTILE),
                          xlk3[:, t * LTILE:(t + 1) * LTILE, :], dsem, waits=w)
                for cc in range(LTILE):
                    c = t * LTILE + cc
                    off_c = float(LOOK - LCH - c * LCH)
                    P.op("dv", lambda h, i=cc, o=off_c: h.tensor_scalar_mul(
                        ofc[:, i:i + 1], llc[:], o),
                        waits=(("ac", v_llc),
                               ("pe", lk_done[t - 1] if t >= 1 else 0)))
                v_of = P.mst("dv")
                for cc in range(LTILE):
                    i = par * 4 + cc
                    wq_ = [("dv", v_of), ("dc", v_er)]
                    if cc == 0:
                        wq_.append(("pe", lk_done[t - 1] if t >= 1 else 0))
                    P.op("ac", lambda h, cc=cc: h.activation(
                        dcT[:, cc * LCH:(cc + 1) * LCH], erows[:], AF.Exp,
                        bias=ofc[:, cc:cc + 1], scale=llc[:, 0:1]),
                        waits=tuple(wq_))
                v_dct = P.mst("ac")
                for cc in range(LTILE):
                    i = par * 4 + cc
                    P.op("pe", lambda h, i=i, cc=cc: h.transpose(
                        ps[6][:, i * 16:(i + 1) * 16],
                        dcT[:, cc * LCH:(cc + 1) * LCH], idfs[0:NH, 0:NH]),
                        waits=(("ac", v_dct), bankf[6]))
                v_tp = P.mst("pe")
                P.op("dv", lambda h, par=par: h.tensor_copy(
                    dcbu[:, par * 64:(par + 1) * 64],
                    ps[6][:, par * 64:(par + 1) * 64]),
                    waits=(("pe", v_tp),))
                v_dcb = P.mst("dv")
                bankf[6] = ("dv", v_dcb)
                for cc in range(LTILE):
                    c = t * LTILE + cc
                    for half_ in range(2):
                        st = (c == 0)
                        w = ((dsem, vld), ("dv", v_dcb)) if (cc == 0 and half_ == 0) else ()
                        P.op("pe", lambda h, cc=cc, half_=half_, st=st, par=par, b=buf: h.matmul(
                            ps[5 - half_][0:NH, 0:512],
                            dcbu[:, par * 64 + cc * 16:par * 64 + (cc + 1) * 16],
                            b[:, cc * 1024 + half_ * 512:cc * 1024 + (half_ + 1) * 512],
                            start=st, stop=(c == NCH - 1),
                            skip_group_check=True),
                            waits=w)
                lk_done.append(P.mst("pe"))
            v_lkpe = lk_done[-1] if lk_done else 0
            if not do_look:
                P.op("dv", lambda h: h.memset(sinit[:], 0.0))
            if do_look:
                for half_ in range(2):
                    P.op("dv", lambda h, half_=half_: h.tensor_copy(
                        dcT[:, 0:512], ps[5 - half_][0:NH, 0:512]),
                        waits=(("pe", v_lkpe),))
                    v_xbt = P.mst("dv")
                    for k4 in range(4):
                        kg = half_ * 4 + k4
                        bank = 6 + (kg % 2)
                        P.op("pe", lambda h, k4=k4, bank=bank: h.transpose(
                            ps[bank][:, 0:16],
                            dcT[0:NH, k4 * 128:(k4 + 1) * 128], idfs[0:NH, 0:NH]),
                            waits=(("dv", v_xbt), bankf[bank]))
                        v_p = P.mst("pe")
                        P.op("dv", lambda h, kg=kg, bank=bank: h.tensor_copy(
                            xbarb[:, kg * 16:(kg + 1) * 16], ps[bank][:, 0:16]),
                            waits=(("pe", v_p),))
                        bankf[bank] = ("dv", P.mst("dv"))
                    if half_ == 0:
                        P.op("dv", lambda h: h.engine_nop() if False else h.nop(),
                             waits=(("pe", P.cnt["pe"]),))
            v_xbar = P.cnt["dv"]
            bankf[5] = ("dv", v_xbar)
            bankf[4] = ("dv", v_xbar)
            cast_weight("q")
            cast_weight("g")
            cast_weight("o")
            v_scopy = 0
            for og in range(8 if do_look else 0):
                for kg in range(8):
                    w = ()
                    if og == 0 and kg == 0:
                        w = (("dwt_v", v_wt["v"]), ("dv", v_xbar))
                    elif kg == 0:
                        w = (("dv", v_scopy),)
                    P.op("pe", lambda h, og=og, kg=kg: h.matmul(
                        ps[4][:, 0:16],
                        wT["v"][:, kg * 1024 + og * 128:kg * 1024 + (og + 1) * 128],
                        xbarb[:, kg * 16:(kg + 1) * 16],
                        start=(kg == 0), stop=(kg == 7)),
                        waits=w)
                v_p = P.mst("pe")
                P.op("dv", lambda h, og=og: h.tensor_copy(
                    sinit[0:64, og:og + 1], ps[4][0:64, 2 * og:2 * og + 1]),
                    waits=(("pe", v_p),))
                P.op("dv", lambda h, og=og: h.tensor_copy(
                    sinit[64:128, og:og + 1], ps[4][64:128, 2 * og + 1:2 * og + 2]))
                v_scopy = P.mst("dv")
            bankf[4] = ("dv", P.cnt["dv"])

            xld = {}
            slot_free = {i: ("dv", 0) for i in range(8)}
            x1o_do = {0: 0, 1: 0}
            v_x1_prev = [0]
            xf_free = {0: ("pe", 0), 1: ("pe", 0)}
            x1slot_ds = {0: 0, 1: 0}
            v_stats_prev = 0
            v_wo_prev = 0
            v_yn_prev = 0

            def load_tile(gt):
                slot = gt % 8
                v = dma("ac", xtm[:, slot * 1024:(slot + 1) * 1024]
                        .rearrange("p (a d) -> p a d", a=1),
                        xin3[:, gt:gt + 1, :], "dx%d" % slot,
                        waits=(slot_free[slot],))
                xld[gt] = ("dx%d" % slot, v)

            for gt in range(8 if att_spans else 0):
                load_tile(gt)

            v_xf_span = {}

            def emit_transposes(s_):
                sl = s_ % 2
                xfsl3_ = xfmb[:, sl * 4096:(sl + 1) * 4096]                    .rearrange("p (kg t) -> p kg t", kg=8)
                bidx = 0
                for half in range(2):
                    hs = 2 * s_ + half
                    for ttl in range(2):
                        gt = 2 * hs + ttl
                        slot = gt % 8
                        for kq in range(2):
                            bank = 4 + (bidx % 2)
                            bidx += 1
                            for j in range(4):
                                kg = kq * 4 + j
                                w = []
                                if j == 0:
                                    w = [xld[gt], bankf[bank]]
                                    if bidx <= 2:
                                        w.append(xf_free[sl])
                                P.op("pe", lambda h, slot=slot, kg=kg, j=j, bank=bank:
                                     h.transpose(
                                         ps[bank][:, j * 128:(j + 1) * 128],
                                         xtm[:, slot * 1024 + kg * 128:
                                             slot * 1024 + (kg + 1) * 128],
                                         idfs[:]),
                                     waits=tuple(w))
                            v_p = P.mst("pe")
                            P.op("dv", lambda h, kq=kq, half=half, ttl=ttl, bank=bank,
                                 x3=xfsl3_: h.tensor_copy(
                                     x3[:, kq * 4:(kq + 1) * 4,
                                        half * 256 + ttl * 128:half * 256 + ttl * 128 + 128],
                                     ps[bank][:, 0:512].rearrange("p (a t) -> p a t", a=4)),
                                 waits=(("pe", v_p),))
                            bankf[bank] = ("dv", P.mst("dv"))
                v_xf_span[s_] = ("dv", P.cnt["dv"])

            if att_spans:
                emit_transposes(0)

            front = {}
            vsp = [0]
            vscan_prev = {0: 0, 1: 0}

            def emit_front(s_, yn_emit=None):
                sl = s_ % 2
                xfsl = xfmb[:, sl * 4096:(sl + 1) * 4096]
                v_xf = v_xf_span[s_]
                F = dict(qmm=[0] * 8, vmm=[0] * 8, gmm=[0] * 8, y=[0] * 8,
                         silu=[0] * 8, sq=[0] * 8, s2mm=[0] * 8, yna=[0] * 8)
                front[s_] = F
                prevF = front.get(s_ - 1)

                def emit_stats(g, F=F):
                    w1_ = [("dv", F["y"][g])]
                    if g == 0:
                        w1_.append(bankf[6])
                    P.op("pe", lambda h, g=g: h.matmul(
                        ps[6][0:1, 0:512], onesc[:, 0:1],
                        stat[:, g * SPAN:(g + 1) * SPAN],
                        start=(g == 0), stop=(g == 7), skip_group_check=True),
                        waits=tuple(w1_))
                    w2_ = [("ac", F["sq"][g])]
                    if g == 0:
                        w2_.append(bankf[7])
                    P.op("pe", lambda h, g=g: h.matmul(
                        ps[7][0:1, 0:512], onesc[:, 0:1],
                        ysq[:, (g % 3) * 512:(g % 3 + 1) * 512],
                        start=(g == 0), stop=(g == 7), skip_group_check=True),
                        waits=tuple(w2_))
                    F["s2mm"][g] = P.mst("pe")

                for g in range(8):
                    if yn_emit is not None:
                        yn_emit(g)
                    for (wTn, boff, vlist) in (("q", 0, F["qmm"]),
                                               ("v", 2, F["vmm"]),
                                               ("g", 4, F["gmm"])):
                        bank = boff + (g % 2)
                        for kg in range(8):
                            w = ()
                            if kg == 0:
                                w = (v_xf, ("dwt_%s" % wTn, v_wt[wTn]), bankf[bank])
                            P.op("pe", lambda h, g=g, kg=kg, wTn=wTn, bank=bank, xfsl=xfsl:
                                 h.matmul(
                                     ps[bank][:, 0:512],
                                     wT[wTn][:, kg * 1024 + g * 128:kg * 1024 + (g + 1) * 128],
                                     xfsl[:, kg * 512:(kg + 1) * 512],
                                     start=(kg == 0), stop=(kg == 7)),
                                 waits=w)
                        vlist[g] = P.mst("pe")
                    vbank = 2 + (g % 2)
                    qbank = g % 2
                    init = sinit if s_ == 0 else scur
                    P.op("dv", lambda h, g=g, vbank=vbank: h.tensor_copy(
                        vsb[:, (g % 2) * 512:(g % 2 + 1) * 512],
                        ps[vbank][:, 0:512]),
                        waits=(("pe", F["vmm"][g]),))
                    v_vev = P.mst("dv")
                    bankf[vbank] = ("dv", v_vev)
                    swaits = [("pe", vsp[0])]
                    if prevF is not None:
                        swaits.append(("ac", prevF["sq"][g]))
                    P.op("dv", lambda h, g=g, init=init: h.tensor_tensor_scan(
                        stat[:, g * SPAN:(g + 1) * SPAN],
                        lamD[:, g:g + 1].broadcast_to([128, SPAN]),
                        vsb[:, (g % 2) * 512:(g % 2 + 1) * 512],
                        init[:, g:g + 1],
                        op0=ALU.mult, op1=ALU.add),
                        waits=tuple(swaits))
                    P.op("dv", lambda h, g=g: h.tensor_copy(
                        scur[:, g:g + 1],
                        stat[:, g * SPAN + SPAN - 1:g * SPAN + SPAN]))
                    v_scan = P.mst("dv")
                    vscan_prev[g % 2] = v_scan
                    P.op("dv", lambda h, g=g, qbank=qbank: h.tensor_tensor(
                        stat[:, g * SPAN:(g + 1) * SPAN],
                        ps[qbank][:, 0:512],
                        stat[:, g * SPAN:(g + 1) * SPAN],
                        ALU.mult),
                        waits=(("pe", F["qmm"][g]),))
                    F["y"][g] = P.mst("dv")
                    bankf[qbank] = ("dv", F["y"][g])
                    gbank = 4 + (g % 2)
                    P.op("ac", lambda h, g=g, gbank=gbank, sl=sl: h.activation(
                        gtb[:, sl * 4096 + g * SPAN:sl * 4096 + (g + 1) * SPAN],
                        ps[gbank][:, 0:512], AF.Silu),
                        waits=(("pe", F["gmm"][g]), ("pe", v_wo_prev)))
                    F["silu"][g] = P.mst("ac")
                    bankf[gbank] = ("ac", F["silu"][g])
                    ws = [("dv", F["y"][g]), ("pe", vsp[0])]
                    if g >= 3:
                        ws.append(("pe", F["s2mm"][g - 3]))
                    elif prevF is not None:
                        ws.append(("pe", prevF["s2mm"][g + 5]))
                    P.op("ac", lambda h, g=g: h.activation(
                        ysq[:, (g % 3) * 512:(g % 3 + 1) * 512],
                        stat[:, g * SPAN:(g + 1) * SPAN], AF.Square),
                        waits=tuple(ws))
                    F["sq"][g] = P.mst("ac")
                    if g >= 2:
                        emit_stats(g - 2)
                    if yn_emit is not None:
                        ln1_step()
                for g in range(6, 8):
                    emit_stats(g)
                vsp[0] = P.cnt["pe"]

            rb_vals = {}

            def emit_rowmath(s2_):
                F2 = front[s2_]
                v_statmm = F2["s2mm"][7]
                P.op("dv", lambda h: h.tensor_scalar_mul(
                    rowb[0:1, 0:512], ps[6][0:1, 0:512], -R1024),
                    waits=(("pe", v_statmm),))
                P.op("dv", lambda h: h.tensor_scalar_mul(
                    rowb[0:1, 512:1024], ps[7][0:1, 0:512], R1024))
                P.op("dv", lambda h: h.tensor_tensor(
                    rowb[0:1, 1024:1536], rowb[0:1, 0:512], rowb[0:1, 0:512],
                    ALU.mult))
                P.op("dv", lambda h: h.tensor_tensor(
                    rowb[0:1, 512:1024], rowb[0:1, 512:1024], rowb[0:1, 1024:1536],
                    ALU.subtract))
                P.op("dv", lambda h: h.tensor_scalar_add(
                    rowb[0:1, 512:1024], rowb[0:1, 512:1024], LN_EPS))
                v_var = P.mst("dv")
                P.op("ac", lambda h: h.activation(
                    rowb[0:1, 1024:1536], rowb[0:1, 512:1024], AF.Sqrt),
                    waits=(("dv", v_var),))
                v_sd = P.mst("ac")
                P.op("dv", lambda h: h.reciprocal(
                    rowb[0:1, 512:1024], rowb[0:1, 1024:1536]),
                    waits=(("ac", v_sd),))
                v_rstd = P.mst("dv")
                P.op("pe", lambda h: h.matmul(
                    ps[6][:, 0:512], onesr[0:1, :], rowb[0:1, 0:512],
                    start=True, stop=True),
                    waits=(("dv", v_var), bankf[6]))
                v_negb = P.mst("pe")
                P.op("pe", lambda h: h.matmul(
                    ps[7][:, 0:512], onesr[0:1, :], rowb[0:1, 512:1024],
                    start=True, stop=True),
                    waits=(("dv", v_rstd), bankf[7]))
                v_rstb = P.mst("pe")
                P.op("dv", lambda h: h.tensor_copy(
                    bcb[:, 0:512], ps[6][:, 0:512]),
                    waits=(("pe", v_negb),))
                P.op("dv", lambda h: h.tensor_copy(
                    bcb[:, 512:1024], ps[7][:, 0:512]),
                    waits=(("pe", v_rstb),))
                v_bcb = P.mst("dv")
                bankf[6] = ("dv", v_bcb)
                bankf[7] = ("dv", v_bcb)
                rb_vals[s2_] = v_bcb

            ln1_gens = []
            v_sqw = [0]

            def emit_ln1(hs, hsl, v_r1):
                v_s2red = 0
                for ttl in range(2):
                    a0 = hsl * 2048 + ttl * 1024
                    P.op("dv", lambda h, a0=a0, ttl=ttl: h.reduce_sum(
                        rcol[:, ttl:ttl + 1], x1st[:, a0:a0 + 1024],
                        AX.X, negate=True),
                        waits=(("po", v_x1_prev[0]),))
                    _v = P.cnt["dv"]
                    P.op("po", lambda h, ttl=ttl: h.tensor_scalar_mul(
                        rcol[:, ttl:ttl + 1], rcol[:, ttl:ttl + 1], R1024),
                        waits=(("dv", _v),))
                    yield
                    P.op("ac", lambda h, a0=a0: h.activation(
                        r1sq[:], x1st[:, a0:a0 + 1024], AF.Square),
                        waits=(("dv", v_r1), ("dv", v_sqw[0])))
                    vsq2 = P.mst("ac")
                    P.op("dv", lambda h, ttl=ttl: h.reduce_sum(
                        rcol[:, 2 + ttl:3 + ttl], r1sq[:], AX.X),
                        waits=(("ac", vsq2),))
                    v_s2red = P.cnt["dv"]
                    v_sqw[0] = v_s2red
                    _v = P.cnt["dv"]
                    P.op("po", lambda h, ttl=ttl: h.tensor_scalar(
                        rcol[:, 2 + ttl:3 + ttl], rcol[:, 2 + ttl:3 + ttl],
                        R1024, None, op0=ALU.mult),
                        waits=(("dv", _v),))
                    P.op("po", lambda h, ttl=ttl: h.tensor_tensor(
                        rcol[:, 6:7], rcol[:, ttl:ttl + 1], rcol[:, ttl:ttl + 1],
                        ALU.mult))
                    P.op("po", lambda h, ttl=ttl: h.tensor_tensor(
                        rcol[:, 4 + ttl:5 + ttl], rcol[:, 2 + ttl:3 + ttl],
                        rcol[:, 6:7], ALU.subtract))
                    if ttl == 0:
                        yield
                P.op("po", lambda h: h.tensor_scalar_add(
                    rcol[:, 4:6], rcol[:, 4:6], LN_EPS))
                v_v = P.mst("po")
                P.op("ac", lambda h: h.activation(
                    rcol[:, 6:8], rcol[:, 4:6], AF.Ln),
                    waits=(("po", v_v),))
                v_sdc = P.mst("ac")
                P.op("po", lambda h: h.tensor_scalar_mul(
                    rcol[:, 6:8], rcol[:, 6:8], -0.5),
                    waits=(("ac", v_sdc),))
                v_hf = P.mst("po")
                P.op("ac", lambda h: h.activation(
                    rcol[:, 6:8], rcol[:, 6:8], AF.Exp),
                    waits=(("po", v_hf),))
                v_sdc = P.mst("ac")
                for ttl in range(2):
                    a0 = hsl * 2048 + ttl * 1024
                    P.op("po", lambda h, a0=a0, ttl=ttl: h.tensor_scalar(
                        x1st[:, a0:a0 + 1024], x1st[:, a0:a0 + 1024],
                        rcol[:, ttl:ttl + 1], rcol[:, 6 + ttl:7 + ttl],
                        op0=ALU.add, op1=ALU.mult),
                        waits=(("ac", v_sdc),))
                v_x1 = P.mst("po")
                v_x1_prev[0] = v_x1
                vz = dma("ac", x1s3[:, 2 * hs:2 * hs + 2, :],
                         x1st[:, hsl * 2048:(hsl + 1) * 2048]
                         .rearrange("p (a d) -> p a d", a=2),
                         "ds%d" % hsl,
                         waits=(("po", v_x1),))
                x1slot_ds[hsl] = vz
                if stage != 2:
                    x1o_do[hsl] = dma(
                        "ac", out3[:, 2 * hs:2 * hs + 2, :],
                        x1st[:, hsl * 2048:(hsl + 1) * 2048]
                        .rearrange("p (a d) -> p a d", a=2),
                        "do%d" % hsl)

            def ln1_step():
                while ln1_gens:
                    try:
                        next(ln1_gens[0])
                        return
                    except StopIteration:
                        ln1_gens.pop(0)

            def ln1_drain():
                while ln1_gens:
                    for _ in ln1_gens.pop(0):
                        pass

            if att_spans:
                emit_front(0)
                emit_rowmath(0)

            x1cast = [0] * NRANGE

            def emit_x1cast(r):
                x1cast[r] = dma(
                    "po", x1b[r * RANGE:(r + 1) * RANGE, :],
                    x1scr[r * RANGE:(r + 1) * RANGE, :],
                    "dxc%d" % r,
                    waits=(("ds0", 16 * (2 * r + 2)), ("ds1", 16 * (2 * r + 2))))

            for s_ in range(att_spans):
                if stage == 2 and s_ in (3, 5, 7):
                    emit_x1cast((s_ - 3) // 2)
                sl = s_ % 2
                F = front[s_]
                v_y = F["y"]
                v_silu = F["silu"]
                v_s2mm = F["s2mm"]
                v_statmm = F["s2mm"][7]

                if s_ + 1 < att_spans:
                    emit_transposes(s_ + 1)

                v_bcb = rb_vals[s_]
                v_y2b = [0] * 8
                v_ynbb = [0] * 8

                def yn_emit(g, F=F, sl=sl, v_bcb=v_bcb, v_y2b=v_y2b,
                            v_ynbb=v_ynbb, v_silu=v_silu):
                    yt = ynts[g % 2]
                    w = [("dv", v_bcb)]
                    if g >= 2:
                        w.append(("po", v_ynbb[g - 2]))
                    P.op("dv", lambda h, g=g, yt=yt: h.tensor_tensor(
                        yt, stat[:, g * SPAN:(g + 1) * SPAN],
                        bcb[:, 0:512], ALU.add),
                        waits=tuple(w))
                    P.op("dv", lambda h, g=g, yt=yt: h.tensor_tensor(
                        yt, yt, bcb[:, 512:1024], ALU.mult))
                    v_mul = P.mst("dv")
                    P.op("po", lambda h, g=g, yt=yt: h.tensor_scalar(
                        ynbb[:], yt,
                        lngc[:, g:g + 1], lnbc[:, g:g + 1],
                        op0=ALU.mult, op1=ALU.add),
                        waits=(("dv", v_mul),))
                    v_ynbb[g] = P.mst("po")
                    P.op("po", lambda h, g=g, sl=sl: h.tensor_tensor(
                        gtb[:, sl * 4096 + g * SPAN:sl * 4096 + (g + 1) * SPAN],
                        ynbb[:],
                        gtb[:, sl * 4096 + g * SPAN:sl * 4096 + (g + 1) * SPAN],
                        ALU.mult),
                        waits=(("ac", v_silu[g]),))
                    v_y2b[g] = P.mst("po")

                if s_ + 1 < att_spans:
                    emit_front(s_ + 1, yn_emit)
                    emit_rowmath(s_ + 1)
                else:
                    for g in range(8):
                        yn_emit(g)
                        ln1_step()

                for half in range(2):
                    hs = 2 * s_ + half
                    hsl = hs % 2
                    for ttl in range(2):
                        tt = half * 2 + ttl
                        gt = 4 * s_ + tt
                        slot = gt % 8
                        for oh in range(2):
                            bank = 2 + oh
                            for dyg in range(8):
                                w = (("po", v_y2b[dyg]),)
                                if dyg == 0:
                                    w = w + (("dwt_o", v_wt["o"]), bankf[bank])
                                P.op("pe", lambda h, tt=tt, oh=oh, dyg=dyg, bank=bank, sl=sl:
                                     h.matmul(
                                         ps[bank][:, 0:512],
                                         gtb[:, sl * 4096 + dyg * SPAN + tt * 128:
                                             sl * 4096 + dyg * SPAN + (tt + 1) * 128],
                                         wT["o"][:, dyg * 1024 + oh * 512:
                                                 dyg * 1024 + (oh + 1) * 512],
                                         start=(dyg == 0), stop=(dyg == 7)),
                                     waits=w)
                            v_wo = P.mst("pe")
                            P.op("dv", lambda h, hsl=hsl, ttl=ttl, oh=oh, slot=slot,
                                 bank=bank: h.tensor_tensor(
                                     x1st[:, hsl * 2048 + ttl * 1024 + oh * 512:
                                          hsl * 2048 + ttl * 1024 + (oh + 1) * 512],
                                     ps[bank][:, 0:512],
                                     xtm[:, slot * 1024 + oh * 512:
                                         slot * 1024 + (oh + 1) * 512],
                                     ALU.add),
                                 waits=(("pe", v_wo),
                                        ("ds%d" % hsl, x1slot_ds[hsl]),
                                        ("do%d" % hsl, x1o_do[hsl])))
                            bankf[bank] = ("dv", P.mst("dv"))
                        slot_free[slot] = ("dv", P.cnt["dv"])
                    ln1_gens.append(emit_ln1(hs, hsl, P.cnt["dv"]))
                v_wo_prev = P.cnt["pe"]
                xf_free[sl] = ("pe", front[s_]["gmm"][7])
                v_yn_prev = P.cnt["dv"]
                for gt in range(4 * s_ + 8, 4 * s_ + 12):
                    if gt < 4 * att_spans and gt not in xld:
                        load_tile(gt)

            pre_drain = {e: P.cnt.get(e, 0) for e in ("pe", "dv", "ac", "po")}
            ln1_drain()

            att_end = {e: P.cnt.get(e, 0) for e in ("pe", "dv", "ac", "po")}
            v_ds_all = P.cnt.get("ds", 0)

            P.mark_cut()
            if stage != 2:
                with nc.Block("attn") as blockA:
                    blockA.tensor(lambda h: P.replay("pe", h, sems, 0))
                    blockA.vector(lambda h: P.replay("dv", h, sems, 0))
                    blockA.scalar(lambda h: P.replay("ac", h, sems, 0))
                    blockA.gpsimd(lambda h: P.replay("po", h, sems, 0))
                    blockA.sync(lambda h: P.replay("sp", h, sems, 0))

        if stage != 2:
            dnames = (["dc", "db1", "dw", "dmeb", "dla", "dlb"]
                      + ["dwc_%s" % n for n in ("q", "v", "g", "o")]
                      + ["dwt_%s" % n for n in ("q", "v", "g", "o")]
                      + ["dx%d" % i for i in range(8)]
                      + ["ds0", "ds1", "do0", "do1"])
            dtot = [(k, P.cnt.get(k, 0)) for k in dnames]
            ctot = [(k, P.cnt.get(k, 0)) for k in ("pe", "dv", "ac")]

            def fin(h):
                for k, v in dtot + ctot:
                    if v > 0:
                        h.wait_ge(sems[k], v)
                if stage in ("setup", "look"):
                    h.dma_start(out=out[0:128, 0:8], in_=sinit[:]).then_inc(sems["do0"], 16)
                    h.wait_ge(sems["do0"], P.cnt.get("do0", 0) + 16)
            with nc.Block("fin") as blockF:
                blockF.sync(fin)
            return nc

        att_dma = [(k, P.cnt.get(k, 0)) for k in
                   (["dc", "db1", "dw", "dla", "dlb", "ds0", "ds1", "do0", "do1"]
                    + ["dwc_%s" % n for n in ("q", "v", "g", "o")]
                    + ["dwt_%s" % n for n in ("q", "v", "g", "o")]
                    + ["dx%d" % i for i in range(8)])
                   if P.cnt.get(k, 0) > 0]
        barrier = tuple([("pe", att_end["pe"]), ("dv", att_end["dv"]),
                         ("ac", att_end["ac"]), ("po", att_end["po"])] + att_dma)
        with ExitStack() as moe:
            x1fmb = sb(moe, "x1fmb", [128, 2 * 8 * 1024], BF16)
            wbuf = [sb(moe, "wb%d" % i, [128, 16384], BF16) for i in range(3)]
            acc = sb(moe, "acc", [128, 8 * 1024], F32)
            hb = sb(moe, "hb", [128, 16 * 512], BF16)
            sqm = sb(moe, "sqm", [128, 512], F32)
            o2st = sb(moe, "o2st", [128, 2048], F32)
            gts = sb(moe, "gts", [128, 8], F32)
            tt1 = sb(moe, "tt1", [128, 8], F32)
            tt2 = sb(moe, "tt2", [128, 8], F32)
            cwsb = sb(moe, "cwsb", [128, 2 * 64], F32)
            lgs = sb(moe, "lgs", [128, 64], F32)
            cwF = sb(moe, "cwF", [NE, 1024], BF16)

            order = []
            for r in range(NRANGE):
                es = list(range(NE)) if r % 2 == 0 else list(range(NE - 1, -1, -1))
                for e in es:
                    order.append((r, e))
            NK = len(order)
            bufW1 = [0] * NK
            bufW2 = [0] * NK
            newload = [False] * NK
            cur1, cur2, nxt = 0, 1, 2
            for k in range(NK):
                if k == 0:
                    newload[0] = True
                    bufW1[0], bufW2[0] = cur1, cur2
                elif order[k][1] == order[k - 1][1]:
                    bufW1[k], bufW2[k] = bufW1[k - 1], bufW2[k - 1]
                    cur1, cur2 = bufW1[k], bufW2[k]
                else:
                    newload[k] = True
                    o1, o2 = cur1, cur2
                    cur1 = nxt
                    cur2 = o1
                    nxt = o2
                    bufW1[k], bufW2[k] = cur1, cur2

            w13 = w1.rearrange("e (kg p) j -> e p kg j", p=128)
            w23 = w2.rearrange("e (jg p) o -> e p jg o", p=128)

            def load_w1(e, buf, waits):
                v = dma("po", wbuf[buf][:].rearrange("p (kg j) -> p kg j", kg=8),
                        w13[e], "dwb%d" % buf, waits=waits)
                return ("dwb%d" % buf, v)

            def load_w2(e, buf, waits):
                v = dma("po", wbuf[buf][:].rearrange("p (jg o) -> p jg o", jg=16),
                        w23[e], "dwb%d" % buf, waits=waits)
                return ("dwb%d" % buf, v)

            v_w1ld = [None] * NK
            v_w2ld = [None] * NK
            nl = [k for k in range(1, NK) if newload[k]]
            w1_after = {}
            for j in range(1, len(nl)):
                kj = nl[j]
                tb = bufW1[kj]
                kprev = max(k2 for k2 in range(kj) if bufW2[k2] == tb)
                w1_after.setdefault(kprev, []).append(kj)

            v_xT = [0] * NRANGE
            v_cwt = [0] * 8
            v_hdone = 0
            v_edone = 0
            v_hbfree = 0
            v_accinit = [0] * NRANGE
            v_ln2 = 0
            v_moe_prev = 0
            v_fmb_user = {0: 0, 1: 0}
            v_cwall = [0] * NRANGE
            v_biasmm = [0]
            o2_do = {0: 0, 1: 0}

            def emit_head_a(r):
                sl = r % 2
                fmb = x1fmb[:, sl * 8192:(sl + 1) * 8192]
                c_sem = "dxc%d" % r
                t0s, t1s = "drb%d" % (2 * (r % 2)), "drb%d" % (2 * (r % 2) + 1)
                for kg in range(8):
                    wl = ()
                    if kg == 0:
                        wl = ((c_sem, x1cast[r]), ("pe", v_fmb_user[sl]))
                        if r == 0:
                            wl = wl + (("pe", front[NSPAN - 1]["vmm"][7]),)
                    dma("sp", fmb[:, kg * 1024:(kg + 1) * 1024],
                        x1b[r * RANGE:(r + 1) * RANGE, kg * 128:(kg + 1) * 128],
                        t0s if kg % 2 == 0 else t1s, waits=wl, transpose=True)
                v_xT[r] = ((t0s, P.cnt[t0s]), (t1s, P.cnt[t1s]))

            def emit_head_b(r):
                sl = r % 2
                fmb = x1fmb[:, sl * 8192:(sl + 1) * 8192]
                for tt in range(8):
                    bank = 6 + (tt % 2)
                    for kg in range(8):
                        w = ()
                        if kg == 0:
                            w = v_xT[r] + (("dmeb", v_meb), bankf[bank])
                        P.op("pe", lambda h, tt=tt, kg=kg, bank=bank, fmb=fmb: h.matmul(
                            ps[bank][:, 0:8],
                            fmb[:, kg * 1024 + tt * 128:kg * 1024 + (tt + 1) * 128],
                            gwb[:, kg * 8:(kg + 1) * 8],
                            start=(kg == 0), stop=(kg == 7)),
                            waits=w)
                    v_lg = P.mst("pe")
                    P.op("dv", lambda h, tt=tt, bank=bank: h.tensor_copy(
                        lgs[:, tt * 8:(tt + 1) * 8], ps[bank][:, 0:8]),
                        waits=(("pe", v_lg),))
                    bankf[bank] = ("dv", P.mst("dv"))
            def emit_cw_chain(r, tt):
                sl = r % 2
                P.op("dv", lambda h, tt=tt: h.reduce_max(
                    rcol[:, 0:1], lgs[:, tt * 8:(tt + 1) * 8], AX.X,
                    negate=True))
                v_nm = P.mst("dv")
                P.op("ac", lambda h, tt=tt: h.activation(
                    gts[:], lgs[:, tt * 8:(tt + 1) * 8], AF.Exp,
                    bias=rcol[:, 0:1]),
                    waits=(("dv", v_nm),))
                v_exp = P.mst("ac")
                P.op("dv", lambda h: h.reduce_sum(rcol[:, 1:2], gts[:], AX.X),
                     waits=(("ac", v_exp),))
                P.op("dv", lambda h: h.reciprocal(rcol[:, 2:3], rcol[:, 1:2]))
                P.op("dv", lambda h: h.tensor_scalar_mul(
                    gts[:], gts[:], rcol[:, 2:3]))
                P.op("dv", lambda h: h.reduce_max(rcol[:, 3:4], gts[:], AX.X))
                P.op("dv", lambda h: h.tensor_scalar(
                    tt1[:], gts[:], rcol[:, 3:4], None, op0=ALU.is_ge))
                P.op("dv", lambda h: h.scalar_tensor_tensor(
                    tt2[:], tt1[:], -BIG, gts[:], op0=ALU.mult, op1=ALU.add))
                P.op("dv", lambda h: h.reduce_max(rcol[:, 4:5], tt2[:], AX.X))
                P.op("dv", lambda h: h.tensor_scalar(
                    tt1[:], gts[:], rcol[:, 4:5], None, op0=ALU.is_ge))
                P.op("dv", lambda h: h.tensor_tensor(
                    rcol[:, 5:6], rcol[:, 3:4], rcol[:, 4:5], ALU.add))
                P.op("dv", lambda h: h.tensor_scalar_add(
                    rcol[:, 5:6], rcol[:, 5:6], 1e-8))
                P.op("dv", lambda h: h.reciprocal(rcol[:, 6:7], rcol[:, 5:6]))
                P.op("dv", lambda h: h.tensor_tensor(
                    gts[:], gts[:], tt1[:], ALU.mult))
                P.op("dv", lambda h, tt=tt, sl=sl: h.tensor_scalar_mul(
                    cwsb[:, sl * 64 + tt * 8:sl * 64 + (tt + 1) * 8],
                    gts[:], rcol[:, 6:7]))
                v_cwt[tt] = P.mst("dv")

            def emit_head_c(r):
                sl = r % 2
                for tt in range(8):
                    bank = 6 + (tt % 2)
                    P.op("pe", lambda h, tt=tt, bank=bank, sl=sl: h.transpose(
                        ps[bank][0:8, 0:128],
                        cwsb[:, sl * 64 + tt * 8:sl * 64 + (tt + 1) * 8], idfs[:]),
                        waits=(("dv", v_cwt[tt]), bankf[bank]))
                    v_p = P.mst("pe")
                    P.op("dv", lambda h, tt=tt, bank=bank: h.tensor_copy(
                        cwF[0:NE, tt * 128:(tt + 1) * 128],
                        ps[bank][0:NE, 0:128]),
                        waits=(("pe", v_p), ("pe", v_biasmm[0])))
                    bankf[bank] = ("dv", P.mst("dv"))
                v_cwall[r] = P.cnt["dv"]

            def emit_ln2_stats(tt):
                a0 = tt * 1024
                P.op("dv", lambda h, a0=a0, tt=tt: h.reduce_sum(
                    rcol[:, tt:tt + 1], acc[:, a0:a0 + 1024], AX.X,
                    negate=True))
                P.op("dv", lambda h, tt=tt: h.tensor_scalar_mul(
                    rcol[:, tt:tt + 1], rcol[:, tt:tt + 1], R1024))
                P.op("dv", lambda h, a0=a0: h.tensor_tensor(
                    sqm[:], acc[:, a0:a0 + 512], acc[:, a0:a0 + 512],
                    ALU.mult))
                P.op("dv", lambda h: h.reduce_sum(gts[:, 0:1], sqm[:], AX.X))
                P.op("dv", lambda h, a0=a0: h.tensor_tensor(
                    sqm[:], acc[:, a0 + 512:a0 + 1024],
                    acc[:, a0 + 512:a0 + 1024], ALU.mult))
                P.op("dv", lambda h: h.reduce_sum(gts[:, 1:2], sqm[:], AX.X))
                P.op("dv", lambda h: h.tensor_tensor(
                    gts[:, 0:1], gts[:, 0:1], gts[:, 1:2], ALU.add))
                P.op("dv", lambda h: h.tensor_scalar_mul(
                    gts[:, 0:1], gts[:, 0:1], R1024))
                P.op("dv", lambda h, tt=tt: h.tensor_tensor(
                    gts[:, 1:2], rcol[:, tt:tt + 1], rcol[:, tt:tt + 1],
                    ALU.mult))
                P.op("dv", lambda h, tt=tt: h.tensor_tensor(
                    rcol2[:, tt:tt + 1], gts[:, 0:1], gts[:, 1:2],
                    ALU.subtract))

            dsf = tuple((k, P.cnt[k]) for k in ("ds0", "ds1") if P.cnt.get(k, 0))
            v_w1ld[0] = load_w1(order[0][1], bufW1[0],
                                (("pe", att_end["pe"]),))
            v_w2ld[0] = load_w2(order[0][1], bufW2[0],
                                (("dv", pre_drain["dv"]),))
            emit_x1cast(NRANGE - 1)
            emit_head_a(0)
            P.op("dv", None, waits=(("po", att_end["po"]),) + dsf)
            P.op("ac", None,
                 waits=(("po", att_end["po"]), ("dv", att_end["dv"])) + dsf)
            P.op("po", None,
                 waits=(("pe", att_end["pe"]), ("dv", att_end["dv"])))
            if nl:
                v_w1ld[nl[0]] = load_w1(order[nl[0]][1], bufW1[nl[0]], ())
            emit_head_b(0)

            for r in range(NRANGE):
                sl = r % 2
                fmb = x1fmb[:, sl * 8192:(sl + 1) * 8192]
                aw = (("ds0", 16 * (2 * r + 2)), ("ds1", 16 * (2 * r + 2)),
                      ("dv", v_ln2))
                if r == 0:
                    aw = aw + barrier
                v_accinit[r] = dma(
                    "sp", acc[:].rearrange("p (a d) -> p a d", a=8),
                    x1s3[:, 8 * r:8 * r + 8, :], "dr",
                    waits=aw)
                for tt8 in range(8):
                    P.op("dv", lambda h, tt8=tt8: h.tensor_scalar_mul(
                        acc[:, tt8 * 1024:(tt8 + 1) * 1024],
                        acc[:, tt8 * 1024:(tt8 + 1) * 1024], 2.0),
                        waits=(("dr", v_accinit[r]),))
                v_acc2 = P.mst("dv")

                base_k = 8 * r
                for ki in range(NE):
                    k = base_k + ki
                    (rr, e) = order[k]
                    for sp2 in range(2):
                        v_h = [0] * 16
                        v_relu = [0] * 16
                        for jg in range(16):
                            bank = jg % 2
                            for kg in range(8):
                                w = ()
                                if kg == 0:
                                    w = (v_w1ld[k], bankf[bank]) + v_xT[r]
                                P.op("pe", lambda h, jg=jg, kg=kg, sp2=sp2, bank=bank,
                                     wb=wbuf[bufW1[k]], fmb=fmb: h.matmul(
                                         ps[bank][:, 0:512],
                                         wb[:, kg * 2048 + jg * 128:kg * 2048 + (jg + 1) * 128],
                                         fmb[:, kg * 1024 + sp2 * 512:kg * 1024 + (sp2 + 1) * 512],
                                         start=(kg == 0), stop=(kg == 7)),
                                     waits=w)
                            v_h[jg] = P.mst("pe")
                            P.op("ac", lambda h, jg=jg, e=e, bank=bank: h.activation(
                                hb[:, jg * 512:(jg + 1) * 512], ps[bank][:, 0:512],
                                AF.Relu, bias=b1T[:, jg * 8 + e:jg * 8 + e + 1]),
                                waits=(("pe", v_h[jg]), ("pe", v_hbfree)))
                            v_relu[jg] = P.mst("ac")
                            bankf[bank] = ("ac", v_relu[jg])
                        v_hdone = P.cnt["pe"]
                        if sp2 == 1 and k + 1 < NK and newload[k + 1]:
                            v_w2ld[k + 1] = load_w2(order[k + 1][1], bufW2[k + 1],
                                                    (("pe", v_hdone),))
                        for tl in range(4):
                            for oh in range(2):
                                bank = 2 + (tl * 2 + oh) % 4
                                gt = sp2 * 4 + tl
                                for jg in range(16):
                                    w2w = []
                                    if jg == 0:
                                        w2w = [v_w2ld[k], bankf[bank],
                                               ("ac", v_relu[3])]
                                    elif jg in (4, 8, 12):
                                        w2w = [("ac", v_relu[min(jg + 3, 15)])]
                                    P.op("pe", lambda h, jg=jg, tl=tl, oh=oh, bank=bank,
                                         wb=wbuf[bufW2[k]]: h.matmul(
                                             ps[bank][:, 0:512],
                                             hb[:, jg * 512 + tl * 128:jg * 512 + (tl + 1) * 128],
                                             wb[:, jg * 1024 + oh * 512:jg * 1024 + (oh + 1) * 512],
                                             start=(jg == 0), stop=(jg == 15)),
                                         waits=tuple(w2w))
                                v_w2g = P.mst("pe")
                                if sp2 == 0:
                                    if ki == 3 and r + 1 < NRANGE:
                                        emit_cw_chain(r + 1, tl * 2 + oh)
                                    elif ki == 0 and r == 0:
                                        emit_cw_chain(0, tl * 2 + oh)
                                P.op("dv", lambda h, gt=gt, oh=oh, e=e, bank=bank, sl=sl:
                                     h.scalar_tensor_tensor(
                                         acc[:, gt * 1024 + oh * 512:gt * 1024 + (oh + 1) * 512],
                                         ps[bank][:, 0:512],
                                         cwsb[:, sl * 64 + gt * 8 + e:sl * 64 + gt * 8 + e + 1],
                                         acc[:, gt * 1024 + oh * 512:gt * 1024 + (oh + 1) * 512],
                                         op0=ALU.mult, op1=ALU.add),
                                     waits=(("pe", v_w2g),))
                                bankf[bank] = ("dv", P.mst("dv"))
                                if ki == NE - 1 and oh == 1:
                                    emit_ln2_stats(gt)
                        v_hbfree = P.cnt["pe"]
                    v_edone = P.cnt["pe"]
                    v_moe_prev = v_edone
                    for kj in w1_after.get(k, ()):
                        v_w1ld[kj] = load_w1(order[kj][1], bufW1[kj],
                                             (("pe", v_edone),))
                    if k + 1 < NK and not newload[k + 1]:
                        v_w1ld[k + 1] = v_w1ld[k]
                        v_w2ld[k + 1] = v_w2ld[k]
                    if r == 0 and ki == 1:
                        emit_head_c(0)
                    if ki == 2:
                        for tt in range(8):
                            for oh in range(2):
                                bank = 6 + ((2 * tt + oh) % 2)
                                P.op("pe", lambda h, tt=tt, oh=oh, bank=bank:
                                     h.matmul(
                                         ps[bank][:, 0:512],
                                         cwF[0:NE, tt * 128:(tt + 1) * 128],
                                         b2b[0:NE, oh * 512:(oh + 1) * 512],
                                         start=True, stop=True),
                                     waits=(("dmeb", v_meb), bankf[bank]))
                                v_p = P.mst("pe")
                                P.op("dv", lambda h, tt=tt, oh=oh, bank=bank:
                                     h.tensor_tensor(
                                         acc[:, tt * 1024 + oh * 512:tt * 1024 + (oh + 1) * 512],
                                         ps[bank][:, 0:512],
                                         acc[:, tt * 1024 + oh * 512:tt * 1024 + (oh + 1) * 512],
                                         ALU.add),
                                     waits=(("pe", v_p),))
                                bankf[bank] = ("dv", P.mst("dv"))
                        v_biasmm[0] = P.cnt["pe"]
                    if r + 1 < NRANGE:
                        if ki == 1:
                            emit_head_a(r + 1)
                        elif ki == 2:
                            emit_head_b(r + 1)
                        elif ki == 4:
                            emit_head_c(r + 1)
                v_fmb_user[sl] = v_edone

                P.op("dv", lambda h: h.tensor_scalar_add(
                    rcol2[:], rcol2[:], LN_EPS))
                v_v = P.mst("dv")
                P.op("ac", lambda h: h.activation(rcol2[:], rcol2[:], AF.Sqrt),
                     waits=(("dv", v_v),))
                v_sdc = P.mst("ac")
                P.op("dv", lambda h: h.reciprocal(rcol2[:], rcol2[:]),
                     waits=(("ac", v_sdc),))
                for tt in range(8):
                    a0 = tt * 1024
                    o2c = (tt % 2) * 1024
                    P.op("dv", lambda h, o2c=o2c, a0=a0, tt=tt: h.tensor_scalar(
                        o2st[:, o2c:o2c + 1024],
                        acc[:, a0:a0 + 1024], rcol[:, tt:tt + 1],
                        rcol2[:, tt:tt + 1],
                        op0=ALU.add, op1=ALU.mult),
                        waits=(("do0", o2_do[0]),))
                    v_o2 = P.mst("dv")
                    if tt % 2 == 1:
                        vdo = dma("sp", out3[:, 8 * r + tt - 1:8 * r + tt + 1, :],
                                  o2st[:].rearrange("p (a d) -> p a d", a=2),
                                  "do0",
                                  waits=(("dv", v_o2),))
                        o2_do[0] = vdo
                v_ln2 = P.cnt["dv"]

            P.op("sp", None, waits=(("do0", P.cnt.get("do0", 0)),
                                    ("do1", P.cnt.get("do1", 0))))


            with nc.Block("moe") as blockB:
                blockB.tensor(lambda h: P.replay("pe", h, sems, None))
                blockB.vector(lambda h: P.replay("dv", h, sems, None))
                blockB.scalar(lambda h: P.replay("ac", h, sems, None))
                blockB.gpsimd(lambda h: P.replay("po", h, sems, None))
                blockB.sync(lambda h: P.replay("sp", h, sems, None))

    return nc


_CACHED = {}


def _get_program(stage):
    if stage not in _CACHED:
        _CACHED[stage] = build_program(stage)
    return _CACHED[stage]


def kernel(L0_attn_x, L0_attn_sin, L0_attn_cos, L0_attn_decay, L0_attn_Wq,
           L0_attn_Wv, L0_attn_Wo, L0_attn_Wg, L0_attn_ln_g, L0_attn_ln_b,
           L0_ffn_gate_W, L0_ffn_W1, L0_ffn_b1, L0_ffn_W2, L0_ffn_b2):
    x = np.ascontiguousarray(L0_attn_x, dtype=np.float32)
    idfv = np.eye(128, dtype=np.float32)
    idbv = np.eye(128).astype(ml_dtypes.bfloat16)
    erv = np.broadcast_to(np.arange(LCH - 1, -1, -1, dtype=np.float32),
                          (NH, LCH)).copy()

    nc = _get_program(STAGE)

    shared = dict(
        wq=np.ascontiguousarray(L0_attn_Wq, dtype=np.float32),
        wv=np.ascontiguousarray(L0_attn_Wv, dtype=np.float32),
        wg=np.ascontiguousarray(L0_attn_Wg, dtype=np.float32),
        wo=np.ascontiguousarray(L0_attn_Wo, dtype=np.float32),
        dec=np.ascontiguousarray(L0_attn_decay, dtype=np.float32),
        lng=np.ascontiguousarray(L0_attn_ln_g, dtype=np.float32),
        lnb=np.ascontiguousarray(L0_attn_ln_b, dtype=np.float32),
        gwt=np.ascontiguousarray(L0_ffn_gate_W, dtype=np.float32),
        w1=np.ascontiguousarray(L0_ffn_W1, dtype=np.float32),
        b1=np.ascontiguousarray(L0_ffn_b1, dtype=np.float32),
        w2=np.ascontiguousarray(L0_ffn_W2, dtype=np.float32),
        b2=np.ascontiguousarray(L0_ffn_b2, dtype=np.float32),
        idf=idfv, idb=idbv, erow=erv,
    )
    in_maps = []
    for c in range(8):
        b, qt = divmod(c, 4)
        xc = np.ascontiguousarray(x[b, qt * TPC:(qt + 1) * TPC])
        xl = np.zeros((LOOK, D), dtype=np.float32)
        if qt > 0:
            n = qt * TPC
            xl[LOOK - n:] = x[b, 0:n]
        m = dict(shared)
        m["xin"] = xc
        m["xlk"] = xl
        in_maps.append(m)

    res = run_bass_kernel_spmd(nc, in_maps, core_ids=list(range(8)))
    outv = np.empty((B, L, D), dtype=np.float32)
    for c in range(8):
        b, qt = divmod(c, 4)
        outv[b, qt * TPC:(qt + 1) * TPC] = res.results[c]["out"]
    return outv


# version 27 (final)
# speedup vs baseline: 1.2006x; 1.0627x over previous
import numpy as np
import ml_dtypes
import concourse.bass as bass
import concourse.mybir as mybir
from concourse.bass_utils import run_bass_kernel_spmd
from contextlib import ExitStack

F32 = mybir.dt.float32
BF16 = mybir.dt.bfloat16
AF = mybir.ActivationFunctionType
ALU = mybir.AluOpType
AX = mybir.AxisListType

B, L, D = 2, 16384, 1024
TPC = 4096
NH = 16
NE, DHID = 8, 2048
SPAN = 512
NSPAN = TPC // SPAN
RANGE = 1024
NRANGE = TPC // RANGE
LOOK = 12288
LCH = 128
LTILE = 4
NLT = LOOK // (LCH * LTILE)
NCH = LOOK // LCH
LN_EPS = 1e-5
BIG = 1e30
R1024 = 1.0 / 1024.0

ENGINES = ("pe", "dv", "ac", "po", "sp")
STAGE = 2


class Plan:
    def __init__(self):
        self.q = {e: [] for e in ENGINES}
        self.cnt = {}
        self.producer = {}
        self.waited = {e: {} for e in ENGINES}
        self.cut = None

    AUTOSYNC = ("dv", "ac", "po")

    def op(self, eng, fn, waits=(), incs=(), is_dma=False):
        auto = (fn is not None) and (not is_dma) and eng in self.AUTOSYNC
        if auto:
            waits = tuple(waits) + ((eng, self.cnt.get(eng, 0)),)
            incs = tuple(incs) + ((eng, 1),)
        fw = []
        for (s, v) in waits:
            if v is None or v <= 0:
                continue
            if self.waited[eng].get(s, 0) >= v:
                continue
            fw.append((s, v))
            self.waited[eng][s] = v
        for (s, amt) in incs:
            self.cnt[s] = self.cnt.get(s, 0) + amt
            p = self.producer.get(s)
            if p is None:
                self.producer[s] = eng
            elif p != eng:
                self.producer[s] = "-"
        self.q[eng].append((tuple(fw), fn, tuple(incs)))

    def mst(self, eng, waits=()):
        assert not waits
        if eng in self.AUTOSYNC:
            return self.cnt[eng]
        q = self.q[eng]
        assert q and q[-1][1] is not None, "mst needs a preceding real op"
        w, fn, incs = q[-1]
        self.cnt[eng] = self.cnt.get(eng, 0) + 1
        q[-1] = (w, fn, incs + ((eng, 1),))
        return self.cnt[eng]

    def mark_cut(self):
        self.cut = {e: len(self.q[e]) for e in ENGINES}

    def replay(self, eng, handle, sems, phase):
        lo = self.cut[eng] if phase == 1 else 0
        hi = self.cut[eng] if phase == 0 else len(self.q[eng])
        for (waits, fn, incs) in self.q[eng][lo:hi]:
            for (s, v) in waits:
                handle.wait_ge(sems[s], v)
            inst = fn(handle) if fn is not None else None
            if incs:
                if inst is None:
                    inst = handle.nop()
                for (s, a) in incs:
                    inst = inst.then_inc(sems[s], a)


def build_program(stage=2):
    att_spans = NSPAN
    do_look = True
    do_moe = (stage == 2)
    if stage == "setup":
        att_spans, do_look = 0, False
    elif stage == "look":
        att_spans = 0
    elif isinstance(stage, tuple) and stage[0] == "spans":
        att_spans = stage[1]
    nc = bass.Bass("TRN2", target_bir_lowering=False)

    xin = nc.dram_tensor("xin", [TPC, D], F32, kind="ExternalInput").ap()
    xlk = nc.dram_tensor("xlk", [LOOK, D], F32, kind="ExternalInput").ap()
    wq = nc.dram_tensor("wq", [D, D], F32, kind="ExternalInput").ap()
    wv = nc.dram_tensor("wv", [D, D], F32, kind="ExternalInput").ap()
    wg = nc.dram_tensor("wg", [D, D], F32, kind="ExternalInput").ap()
    wo = nc.dram_tensor("wo", [D, D], F32, kind="ExternalInput").ap()
    dec = nc.dram_tensor("dec", [NH, 2, 1], F32, kind="ExternalInput").ap()
    lng = nc.dram_tensor("lng", [D], F32, kind="ExternalInput").ap()
    lnb = nc.dram_tensor("lnb", [D], F32, kind="ExternalInput").ap()
    gwt = nc.dram_tensor("gwt", [D, NE], F32, kind="ExternalInput").ap()
    w1 = nc.dram_tensor("w1", [NE, D, DHID], F32, kind="ExternalInput").ap()
    b1 = nc.dram_tensor("b1", [NE, DHID], F32, kind="ExternalInput").ap()
    w2 = nc.dram_tensor("w2", [NE, DHID, D], F32, kind="ExternalInput").ap()
    b2 = nc.dram_tensor("b2", [NE, D], F32, kind="ExternalInput").ap()
    idf = nc.dram_tensor("idf", [128, 128], F32, kind="ExternalInput").ap()
    idb = nc.dram_tensor("idb", [128, 128], BF16, kind="ExternalInput").ap()
    erow = nc.dram_tensor("erow", [NH, LCH], F32, kind="ExternalInput").ap()
    out = nc.dram_tensor("out", [TPC, D], F32, kind="ExternalOutput").ap()

    x1scr = nc.dram_tensor("x1scr", [TPC, D], F32, kind="Internal").ap()
    x1b = nc.dram_tensor("x1b", [TPC, D], BF16, kind="Internal").ap()
    wtscr = {nm: nc.dram_tensor("wtscr_%s" % nm, [D, D], BF16,
                                 kind="Internal").ap()
             for nm in ("q", "v", "g", "o")}

    xin3 = xin.rearrange("(a p) d -> p a d", p=128)
    out3 = out.rearrange("(a p) d -> p a d", p=128)
    x1s3 = x1scr.rearrange("(a p) d -> p a d", p=128)
    xlk3 = xlk.rearrange("(c p) d -> p c d", p=128)

    sems = {}
    P = Plan()

    with ExitStack() as outer:
        semnames = (["dc", "db1", "dw", "dmeb", "dr"]
                    + ["dwc_%s" % n for n in ("q", "v", "g", "o")]
                    + ["dxc%d" % i for i in range(4)]
                    + ["dwt_%s" % n for n in ("q", "v", "g", "o")]
                    + ["dx%d" % i for i in range(8)]
                    + ["dla", "dlb", "ds0", "ds1", "do0", "do1"]
                    + ["dwb%d" % i for i in range(3)]
                    + ["drb%d" % i for i in range(4)]
                    + ["pe", "dv", "ac", "po", "sp"])
        for s in semnames:
            sems[s] = outer.enter_context(nc.semaphore(s))

        def sb(stack, name, shape, dt):
            return stack.enter_context(nc.sbuf_tensor(name, shape, dt))

        idfs = sb(outer, "idfs", [128, 128], F32)
        idbs = sb(outer, "idbs", [128, 128], BF16)
        decs = sb(outer, "decs", [NH, 2], F32)
        lamc = sb(outer, "lamc", [NH, 1], F32)
        llc = sb(outer, "llc", [NH, 1], F32)
        lamr = sb(outer, "lamr", [1, NH], F32)
        lamD = sb(outer, "lamD", [128, 8], F32)
        sinit = sb(outer, "sinit", [128, 8], F32)
        scur = sb(outer, "scur", [128, 8], F32)
        lngc = sb(outer, "lngc", [128, 8], F32)
        lnbc = sb(outer, "lnbc", [128, 8], F32)
        ln8 = sb(outer, "ln8", [8, 256], F32)
        onesc = sb(outer, "onesc", [128, 1], F32)
        onesr = sb(outer, "onesr", [1, 128], F32)
        erows = sb(outer, "erows", [NH, LCH], F32)
        b1s2 = sb(outer, "b1s2", [NE, 512], F32)
        b1T = sb(outer, "b1T", [128, 128], F32)
        b2b = sb(outer, "b2b", [NE, D], BF16)
        gwb = sb(outer, "gwb", [128, 64], BF16)
        rowb = sb(outer, "rowb", [1, 3 * SPAN], F32)
        rcol = sb(outer, "rcol", [128, 8], F32)
        rcol2 = sb(outer, "rcol2", [128, 8], F32)

        ps = [outer.enter_context(nc.psum_tensor("ps%d" % i, [128, 512], F32))
              for i in range(8)]
        bankf = {i: ("dv", 0) for i in range(8)}

        def dma(eng, dst, src, s, waits=(), transpose=False):
            if transpose:
                fn = lambda h, d=dst, r=src: h.dma_start(out=d, in_=r, transpose=True)
            else:
                fn = lambda h, d=dst, r=src: h.dma_start(out=d, in_=r)
            P.op(eng, fn, waits=waits, incs=((s, 16),), is_dma=True)
            return P.cnt[s]

        dma("ac", idfs[:], idf, "dc")
        dma("ac", idbs[:], idb, "dc")
        dma("ac", decs[:], dec.rearrange("h a b -> h (a b)"), "dc")
        dma("ac", ln8[:, 0:128], lng.rearrange("(a b) -> a b", b=128), "dc")
        dma("ac", ln8[:, 128:256], lnb.rearrange("(a b) -> a b", b=128), "dc")
        v_dc = dma("ac", erows[:], erow, "dc")
        v_idf = v_dec = v_ln = v_er = v_dc
        dma("po", b2b[:], b2, "dmeb")
        v_meb = dma("po", gwb[:].rearrange("p (kg e) -> p kg e", kg=8),
                    gwt.rearrange("(kg p) e -> p kg e", p=128), "dmeb")
        v_b2 = v_gw = v_meb

        P.op("dv", lambda h: h.memset(onesc[:], 1.0))
        P.op("dv", lambda h: h.memset(onesr[:], 1.0))
        P.op("dv", lambda h: h.tensor_scalar_max(lamc[:], decs[:, 1:2], 1e-30),
             waits=(("dc", v_dec),))
        v_lamc = P.mst("dv")
        P.op("ac", lambda h: h.activation(llc[:], lamc[:], AF.Ln),
             waits=(("dv", v_lamc),))
        v_llc = P.mst("ac")
        P.op("pe", lambda h: h.transpose(ps[6][0:1, 0:NH], lamc[:], idfs[0:NH, 0:NH]),
             waits=(("dv", v_lamc), ("dc", v_idf)))
        v_p = P.mst("pe")
        P.op("dv", lambda h: h.tensor_copy(lamr[:], ps[6][0:1, 0:NH]),
             waits=(("pe", v_p),))
        v_lamr = P.mst("dv")
        P.op("pe", lambda h: h.matmul(ps[6][:, 0:NH], onesr[0:1, :], lamr[0:1, :],
                                      start=True, stop=True),
             waits=(("dv", v_lamr),))
        v_p = P.mst("pe")
        for g in range(8):
            P.op("dv", lambda h, g=g: h.tensor_copy(
                lamD[0:64, g:g + 1], ps[6][0:64, 2 * g:2 * g + 1]),
                waits=(("pe", v_p),))
            P.op("dv", lambda h, g=g: h.tensor_copy(
                lamD[64:128, g:g + 1], ps[6][64:128, 2 * g + 1:2 * g + 2]))
        v_lamD = P.mst("dv")
        bankf[6] = ("dv", v_lamD)
        P.op("pe", lambda h: h.transpose(ps[7][:, 0:8], ln8[:, 0:128],
                                         idfs[0:8, 0:8]), waits=(("dc", v_ln),))
        P.op("pe", lambda h: h.transpose(ps[7][:, 8:16], ln8[:, 128:256],
                                         idfs[0:8, 0:8]))
        v_p = P.mst("pe")
        P.op("dv", lambda h: h.tensor_copy(lngc[:], ps[7][:, 0:8]),
             waits=(("pe", v_p),))
        P.op("dv", lambda h: h.tensor_copy(lnbc[:], ps[7][:, 8:16]))
        P.mst("dv")
        for bc in range(4):
            v_b1 = dma("ac", b1s2[:], b1[:, bc * 512:(bc + 1) * 512], "db1",
                       waits=(("pe", P.cnt["pe"]),))
            for j in range(4):
                jg = bc * 4 + j
                P.op("pe", lambda h, j=j: h.transpose(
                    ps[7][:, 0:8], b1s2[0:NE, j * 128:(j + 1) * 128],
                    idfs[0:NE, 0:NE]),
                    waits=(("db1", v_b1), ("dv", P.cnt["dv"])))
                v_p = P.mst("pe")
                P.op("dv", lambda h, jg=jg: h.tensor_copy(
                    b1T[:, jg * 8:(jg + 1) * 8], ps[7][:, 0:8]),
                    waits=(("pe", v_p),))
                P.mst("dv")
        bankf[7] = ("dv", P.cnt["dv"])

        with ExitStack() as att:
            wT = {nm: sb(att, "w%sT" % nm, [128, 8 * 1024], BF16)
                  for nm in ("q", "v", "g", "o")}
            xtm = sb(att, "xtm", [128, 8 * 1024], F32)
            xfmb = sb(att, "xfmb", [128, 2 * 4096], BF16)
            xlt = sb(att, "xlt", [128, 2 * 4096], BF16)
            dcT = sb(att, "dcT", [NH, 4 * LCH], F32)
            ofc = sb(att, "ofc", [NH, 8], F32)
            dcbu = sb(att, "dcbu", [128, 2 * 64], BF16)
            xbarb = sb(att, "xbarb", [128, 128], BF16)
            stat = sb(att, "stat", [128, 8 * SPAN], F32)
            ysq = sb(att, "ysq", [128, 3 * 512], F32)
            ynt = sb(att, "ynt", [128, 512], F32)
            ynbb = sb(att, "ynbb", [128, 512], BF16)
            vsb = sb(att, "vsb", [128, 2 * 512], BF16)
            gtb = sb(att, "gtb", [128, 2 * 8 * SPAN], BF16)
            r1sq = sb(att, "r1sq", [128, 1024], F32)
            x1st = sb(att, "x1st", [128, 2 * 2048], F32)
            bcb = xlt[:, 0:2048].bitcast(F32)
            ynts = [ynt[:, 0:512], xlt[:, 2048:3072].bitcast(F32)]

            v_wt = {}

            wcv = {}

            def cast_weight_dram(nm, src):
                wcv[nm] = dma("po", wtscr[nm], src, "dwc_%s" % nm)

            def cast_weight(nm):
                for kg in range(8):
                    dma("sp", wT[nm][:, kg * 1024:(kg + 1) * 1024],
                        wtscr[nm][:, kg * 128:(kg + 1) * 128], "dwt_%s" % nm,
                        waits=(("dwc_%s" % nm, wcv[nm]),) if kg == 0 else (),
                        transpose=True)
                v_wt[nm] = P.cnt["dwt_%s" % nm]

            cast_weight_dram("v", wv)
            cast_weight_dram("q", wq)
            cast_weight_dram("g", wg)
            cast_weight_dram("o", wo)
            cast_weight("v")

            lk_done = []
            for t in range(NLT if do_look else 0):
                par = t % 2
                buf = xlt[:, par * 4096:(par + 1) * 4096]
                w = (("pe", lk_done[t - 2]),) if t >= 2 else ()
                eng, dsem = ("po", "dla") if par == 0 else ("po", "dlb")
                vld = dma(eng, buf.rearrange("p (c d) -> p c d", c=LTILE),
                          xlk3[:, t * LTILE:(t + 1) * LTILE, :], dsem, waits=w)
                for cc in range(LTILE):
                    c = t * LTILE + cc
                    off_c = float(LOOK - LCH - c * LCH)
                    P.op("dv", lambda h, i=cc, o=off_c: h.tensor_scalar_mul(
                        ofc[:, i:i + 1], llc[:], o),
                        waits=(("ac", v_llc),
                               ("pe", lk_done[t - 1] if t >= 1 else 0)))
                v_of = P.mst("dv")
                for cc in range(LTILE):
                    i = par * 4 + cc
                    wq_ = [("dv", v_of), ("dc", v_er)]
                    if cc == 0:
                        wq_.append(("pe", lk_done[t - 1] if t >= 1 else 0))
                    P.op("ac", lambda h, cc=cc: h.activation(
                        dcT[:, cc * LCH:(cc + 1) * LCH], erows[:], AF.Exp,
                        bias=ofc[:, cc:cc + 1], scale=llc[:, 0:1]),
                        waits=tuple(wq_))
                v_dct = P.mst("ac")
                for cc in range(LTILE):
                    i = par * 4 + cc
                    P.op("pe", lambda h, i=i, cc=cc: h.transpose(
                        ps[6][:, i * 16:(i + 1) * 16],
                        dcT[:, cc * LCH:(cc + 1) * LCH], idfs[0:NH, 0:NH]),
                        waits=(("ac", v_dct), bankf[6]))
                v_tp = P.mst("pe")
                P.op("dv", lambda h, par=par: h.tensor_copy(
                    dcbu[:, par * 64:(par + 1) * 64],
                    ps[6][:, par * 64:(par + 1) * 64]),
                    waits=(("pe", v_tp),))
                v_dcb = P.mst("dv")
                bankf[6] = ("dv", v_dcb)
                for cc in range(LTILE):
                    c = t * LTILE + cc
                    for half_ in range(2):
                        st = (c == 0)
                        w = ((dsem, vld), ("dv", v_dcb)) if (cc == 0 and half_ == 0) else ()
                        P.op("pe", lambda h, cc=cc, half_=half_, st=st, par=par, b=buf: h.matmul(
                            ps[5 - half_][0:NH, 0:512],
                            dcbu[:, par * 64 + cc * 16:par * 64 + (cc + 1) * 16],
                            b[:, cc * 1024 + half_ * 512:cc * 1024 + (half_ + 1) * 512],
                            start=st, stop=(c == NCH - 1),
                            skip_group_check=True),
                            waits=w)
                lk_done.append(P.mst("pe"))
            v_lkpe = lk_done[-1] if lk_done else 0
            if not do_look:
                P.op("dv", lambda h: h.memset(sinit[:], 0.0))
            if do_look:
                for half_ in range(2):
                    P.op("dv", lambda h, half_=half_: h.tensor_copy(
                        dcT[:, 0:512], ps[5 - half_][0:NH, 0:512]),
                        waits=(("pe", v_lkpe),))
                    v_xbt = P.mst("dv")
                    for k4 in range(4):
                        kg = half_ * 4 + k4
                        bank = 6 + (kg % 2)
                        P.op("pe", lambda h, k4=k4, bank=bank: h.transpose(
                            ps[bank][:, 0:16],
                            dcT[0:NH, k4 * 128:(k4 + 1) * 128], idfs[0:NH, 0:NH]),
                            waits=(("dv", v_xbt), bankf[bank]))
                        v_p = P.mst("pe")
                        P.op("dv", lambda h, kg=kg, bank=bank: h.tensor_copy(
                            xbarb[:, kg * 16:(kg + 1) * 16], ps[bank][:, 0:16]),
                            waits=(("pe", v_p),))
                        bankf[bank] = ("dv", P.mst("dv"))
                    if half_ == 0:
                        P.op("dv", lambda h: h.engine_nop() if False else h.nop(),
                             waits=(("pe", P.cnt["pe"]),))
            v_xbar = P.cnt["dv"]
            bankf[5] = ("dv", v_xbar)
            bankf[4] = ("dv", v_xbar)
            cast_weight("q")
            cast_weight("g")
            cast_weight("o")
            v_scopy = 0
            for og in range(8 if do_look else 0):
                for kg in range(8):
                    w = ()
                    if og == 0 and kg == 0:
                        w = (("dwt_v", v_wt["v"]), ("dv", v_xbar))
                    elif kg == 0:
                        w = (("dv", v_scopy),)
                    P.op("pe", lambda h, og=og, kg=kg: h.matmul(
                        ps[4][:, 0:16],
                        wT["v"][:, kg * 1024 + og * 128:kg * 1024 + (og + 1) * 128],
                        xbarb[:, kg * 16:(kg + 1) * 16],
                        start=(kg == 0), stop=(kg == 7)),
                        waits=w)
                v_p = P.mst("pe")
                P.op("dv", lambda h, og=og: h.tensor_copy(
                    sinit[0:64, og:og + 1], ps[4][0:64, 2 * og:2 * og + 1]),
                    waits=(("pe", v_p),))
                P.op("dv", lambda h, og=og: h.tensor_copy(
                    sinit[64:128, og:og + 1], ps[4][64:128, 2 * og + 1:2 * og + 2]))
                v_scopy = P.mst("dv")
            bankf[4] = ("dv", P.cnt["dv"])

            xld = {}
            slot_free = {i: ("dv", 0) for i in range(8)}
            x1o_do = {0: 0, 1: 0}
            v_x1_prev = [0]
            xf_free = {0: ("pe", 0), 1: ("pe", 0)}
            x1slot_ds = {0: 0, 1: 0}
            v_stats_prev = 0
            v_wo_prev = 0
            v_yn_prev = 0

            def load_tile(gt):
                slot = gt % 8
                v = dma("ac", xtm[:, slot * 1024:(slot + 1) * 1024]
                        .rearrange("p (a d) -> p a d", a=1),
                        xin3[:, gt:gt + 1, :], "dx%d" % slot,
                        waits=(slot_free[slot],))
                xld[gt] = ("dx%d" % slot, v)

            for gt in range(8 if att_spans else 0):
                load_tile(gt)

            v_xf_span = {}

            def emit_transposes(s_):
                sl = s_ % 2
                xfsl3_ = xfmb[:, sl * 4096:(sl + 1) * 4096]                    .rearrange("p (kg t) -> p kg t", kg=8)
                bidx = 0
                for half in range(2):
                    hs = 2 * s_ + half
                    for ttl in range(2):
                        gt = 2 * hs + ttl
                        slot = gt % 8
                        for kq in range(2):
                            bank = 4 + (bidx % 2)
                            bidx += 1
                            for j in range(4):
                                kg = kq * 4 + j
                                w = []
                                if j == 0:
                                    w = [xld[gt], bankf[bank]]
                                    if bidx <= 2:
                                        w.append(xf_free[sl])
                                P.op("pe", lambda h, slot=slot, kg=kg, j=j, bank=bank:
                                     h.transpose(
                                         ps[bank][:, j * 128:(j + 1) * 128],
                                         xtm[:, slot * 1024 + kg * 128:
                                             slot * 1024 + (kg + 1) * 128],
                                         idfs[:]),
                                     waits=tuple(w))
                            v_p = P.mst("pe")
                            P.op("dv", lambda h, kq=kq, half=half, ttl=ttl, bank=bank,
                                 x3=xfsl3_: h.tensor_copy(
                                     x3[:, kq * 4:(kq + 1) * 4,
                                        half * 256 + ttl * 128:half * 256 + ttl * 128 + 128],
                                     ps[bank][:, 0:512].rearrange("p (a t) -> p a t", a=4)),
                                 waits=(("pe", v_p),))
                            bankf[bank] = ("dv", P.mst("dv"))
                v_xf_span[s_] = ("dv", P.cnt["dv"])

            if att_spans:
                emit_transposes(0)

            front = {}
            vsp = [0]
            vscan_prev = {0: 0, 1: 0}

            def emit_front(s_, yn_emit=None):
                sl = s_ % 2
                xfsl = xfmb[:, sl * 4096:(sl + 1) * 4096]
                v_xf = v_xf_span[s_]
                F = dict(qmm=[0] * 8, vmm=[0] * 8, gmm=[0] * 8, y=[0] * 8,
                         silu=[0] * 8, sq=[0] * 8, s2mm=[0] * 8, yna=[0] * 8)
                front[s_] = F
                prevF = front.get(s_ - 1)

                def emit_stats(g, F=F):
                    w1_ = [("dv", F["y"][g])]
                    if g == 0:
                        w1_.append(bankf[6])
                    P.op("pe", lambda h, g=g: h.matmul(
                        ps[6][0:1, 0:512], onesc[:, 0:1],
                        stat[:, g * SPAN:(g + 1) * SPAN],
                        start=(g == 0), stop=(g == 7), skip_group_check=True),
                        waits=tuple(w1_))
                    w2_ = [("ac", F["sq"][g])]
                    if g == 0:
                        w2_.append(bankf[7])
                    P.op("pe", lambda h, g=g: h.matmul(
                        ps[7][0:1, 0:512], onesc[:, 0:1],
                        ysq[:, (g % 3) * 512:(g % 3 + 1) * 512],
                        start=(g == 0), stop=(g == 7), skip_group_check=True),
                        waits=tuple(w2_))
                    F["s2mm"][g] = P.mst("pe")

                for g in range(8):
                    if yn_emit is not None:
                        yn_emit(g)
                    for (wTn, boff, vlist) in (("q", 0, F["qmm"]),
                                               ("v", 2, F["vmm"]),
                                               ("g", 4, F["gmm"])):
                        bank = boff + (g % 2)
                        for kg in range(8):
                            w = ()
                            if kg == 0:
                                w = (v_xf, ("dwt_%s" % wTn, v_wt[wTn]), bankf[bank])
                            P.op("pe", lambda h, g=g, kg=kg, wTn=wTn, bank=bank, xfsl=xfsl:
                                 h.matmul(
                                     ps[bank][:, 0:512],
                                     wT[wTn][:, kg * 1024 + g * 128:kg * 1024 + (g + 1) * 128],
                                     xfsl[:, kg * 512:(kg + 1) * 512],
                                     start=(kg == 0), stop=(kg == 7)),
                                 waits=w)
                        vlist[g] = P.mst("pe")
                    vbank = 2 + (g % 2)
                    qbank = g % 2
                    init = sinit if s_ == 0 else scur
                    P.op("dv", lambda h, g=g, vbank=vbank: h.tensor_copy(
                        vsb[:, (g % 2) * 512:(g % 2 + 1) * 512],
                        ps[vbank][:, 0:512]),
                        waits=(("pe", F["vmm"][g]),))
                    v_vev = P.mst("dv")
                    bankf[vbank] = ("dv", v_vev)
                    swaits = [("pe", vsp[0])]
                    if prevF is not None:
                        swaits.append(("ac", prevF["sq"][g]))
                    P.op("dv", lambda h, g=g, init=init: h.tensor_tensor_scan(
                        stat[:, g * SPAN:(g + 1) * SPAN],
                        lamD[:, g:g + 1].broadcast_to([128, SPAN]),
                        vsb[:, (g % 2) * 512:(g % 2 + 1) * 512],
                        init[:, g:g + 1],
                        op0=ALU.mult, op1=ALU.add),
                        waits=tuple(swaits))
                    P.op("dv", lambda h, g=g: h.tensor_copy(
                        scur[:, g:g + 1],
                        stat[:, g * SPAN + SPAN - 1:g * SPAN + SPAN]))
                    v_scan = P.mst("dv")
                    vscan_prev[g % 2] = v_scan
                    P.op("dv", lambda h, g=g, qbank=qbank: h.tensor_tensor(
                        stat[:, g * SPAN:(g + 1) * SPAN],
                        ps[qbank][:, 0:512],
                        stat[:, g * SPAN:(g + 1) * SPAN],
                        ALU.mult),
                        waits=(("pe", F["qmm"][g]),))
                    F["y"][g] = P.mst("dv")
                    bankf[qbank] = ("dv", F["y"][g])
                    gbank = 4 + (g % 2)
                    P.op("ac", lambda h, g=g, gbank=gbank, sl=sl: h.activation(
                        gtb[:, sl * 4096 + g * SPAN:sl * 4096 + (g + 1) * SPAN],
                        ps[gbank][:, 0:512], AF.Silu),
                        waits=(("pe", F["gmm"][g]), ("pe", v_wo_prev)))
                    F["silu"][g] = P.mst("ac")
                    bankf[gbank] = ("ac", F["silu"][g])
                    ws = [("dv", F["y"][g]), ("pe", vsp[0])]
                    if g >= 3:
                        ws.append(("pe", F["s2mm"][g - 3]))
                    elif prevF is not None:
                        ws.append(("pe", prevF["s2mm"][g + 5]))
                    P.op("ac", lambda h, g=g: h.activation(
                        ysq[:, (g % 3) * 512:(g % 3 + 1) * 512],
                        stat[:, g * SPAN:(g + 1) * SPAN], AF.Square),
                        waits=tuple(ws))
                    F["sq"][g] = P.mst("ac")
                    if g >= 2:
                        emit_stats(g - 2)
                    if yn_emit is not None:
                        ln1_step()
                for g in range(6, 8):
                    emit_stats(g)
                vsp[0] = P.cnt["pe"]

            rb_vals = {}

            def emit_rowmath(s2_):
                F2 = front[s2_]
                v_statmm = F2["s2mm"][7]
                P.op("dv", lambda h: h.tensor_scalar_mul(
                    rowb[0:1, 0:512], ps[6][0:1, 0:512], -R1024),
                    waits=(("pe", v_statmm),))
                P.op("dv", lambda h: h.tensor_scalar_mul(
                    rowb[0:1, 512:1024], ps[7][0:1, 0:512], R1024))
                P.op("dv", lambda h: h.tensor_tensor(
                    rowb[0:1, 1024:1536], rowb[0:1, 0:512], rowb[0:1, 0:512],
                    ALU.mult))
                P.op("dv", lambda h: h.tensor_tensor(
                    rowb[0:1, 512:1024], rowb[0:1, 512:1024], rowb[0:1, 1024:1536],
                    ALU.subtract))
                P.op("dv", lambda h: h.tensor_scalar_add(
                    rowb[0:1, 512:1024], rowb[0:1, 512:1024], LN_EPS))
                v_var = P.mst("dv")
                P.op("ac", lambda h: h.activation(
                    rowb[0:1, 1024:1536], rowb[0:1, 512:1024], AF.Sqrt),
                    waits=(("dv", v_var),))
                v_sd = P.mst("ac")
                P.op("dv", lambda h: h.reciprocal(
                    rowb[0:1, 512:1024], rowb[0:1, 1024:1536]),
                    waits=(("ac", v_sd),))
                v_rstd = P.mst("dv")
                P.op("pe", lambda h: h.matmul(
                    ps[6][:, 0:512], onesr[0:1, :], rowb[0:1, 0:512],
                    start=True, stop=True),
                    waits=(("dv", v_var), bankf[6]))
                v_negb = P.mst("pe")
                P.op("pe", lambda h: h.matmul(
                    ps[7][:, 0:512], onesr[0:1, :], rowb[0:1, 512:1024],
                    start=True, stop=True),
                    waits=(("dv", v_rstd), bankf[7]))
                v_rstb = P.mst("pe")
                P.op("dv", lambda h: h.tensor_copy(
                    bcb[:, 0:512], ps[6][:, 0:512]),
                    waits=(("pe", v_negb),))
                P.op("dv", lambda h: h.tensor_copy(
                    bcb[:, 512:1024], ps[7][:, 0:512]),
                    waits=(("pe", v_rstb),))
                v_bcb = P.mst("dv")
                bankf[6] = ("dv", v_bcb)
                bankf[7] = ("dv", v_bcb)
                rb_vals[s2_] = v_bcb

            ln1_gens = []
            v_sqw = [0]

            def emit_ln1(hs, hsl, v_r1):
                v_s2red = 0
                for ttl in range(2):
                    a0 = hsl * 2048 + ttl * 1024
                    P.op("dv", lambda h, a0=a0, ttl=ttl: h.reduce_sum(
                        rcol[:, ttl:ttl + 1], x1st[:, a0:a0 + 1024],
                        AX.X, negate=True),
                        waits=(("po", v_x1_prev[0]),))
                    _v = P.cnt["dv"]
                    P.op("po", lambda h, ttl=ttl: h.tensor_scalar_mul(
                        rcol[:, ttl:ttl + 1], rcol[:, ttl:ttl + 1], R1024),
                        waits=(("dv", _v),))
                    yield
                    P.op("ac", lambda h, a0=a0: h.activation(
                        r1sq[:], x1st[:, a0:a0 + 1024], AF.Square),
                        waits=(("dv", v_r1), ("dv", v_sqw[0])))
                    vsq2 = P.mst("ac")
                    P.op("dv", lambda h, ttl=ttl: h.reduce_sum(
                        rcol[:, 2 + ttl:3 + ttl], r1sq[:], AX.X),
                        waits=(("ac", vsq2),))
                    v_s2red = P.cnt["dv"]
                    v_sqw[0] = v_s2red
                    _v = P.cnt["dv"]
                    P.op("po", lambda h, ttl=ttl: h.tensor_scalar(
                        rcol[:, 2 + ttl:3 + ttl], rcol[:, 2 + ttl:3 + ttl],
                        R1024, None, op0=ALU.mult),
                        waits=(("dv", _v),))
                    P.op("po", lambda h, ttl=ttl: h.tensor_tensor(
                        rcol[:, 6:7], rcol[:, ttl:ttl + 1], rcol[:, ttl:ttl + 1],
                        ALU.mult))
                    P.op("po", lambda h, ttl=ttl: h.tensor_tensor(
                        rcol[:, 4 + ttl:5 + ttl], rcol[:, 2 + ttl:3 + ttl],
                        rcol[:, 6:7], ALU.subtract))
                    if ttl == 0:
                        yield
                P.op("po", lambda h: h.tensor_scalar_add(
                    rcol[:, 4:6], rcol[:, 4:6], LN_EPS))
                v_v = P.mst("po")
                P.op("ac", lambda h: h.activation(
                    rcol[:, 6:8], rcol[:, 4:6], AF.Ln),
                    waits=(("po", v_v),))
                v_sdc = P.mst("ac")
                P.op("po", lambda h: h.tensor_scalar_mul(
                    rcol[:, 6:8], rcol[:, 6:8], -0.5),
                    waits=(("ac", v_sdc),))
                v_hf = P.mst("po")
                P.op("ac", lambda h: h.activation(
                    rcol[:, 6:8], rcol[:, 6:8], AF.Exp),
                    waits=(("po", v_hf),))
                v_sdc = P.mst("ac")
                for ttl in range(2):
                    a0 = hsl * 2048 + ttl * 1024
                    P.op("po", lambda h, a0=a0, ttl=ttl: h.tensor_scalar(
                        x1st[:, a0:a0 + 1024], x1st[:, a0:a0 + 1024],
                        rcol[:, ttl:ttl + 1], rcol[:, 6 + ttl:7 + ttl],
                        op0=ALU.add, op1=ALU.mult),
                        waits=(("ac", v_sdc),))
                v_x1 = P.mst("po")
                v_x1_prev[0] = v_x1
                vz = dma("ac", x1s3[:, 2 * hs:2 * hs + 2, :],
                         x1st[:, hsl * 2048:(hsl + 1) * 2048]
                         .rearrange("p (a d) -> p a d", a=2),
                         "ds%d" % hsl,
                         waits=(("po", v_x1),))
                x1slot_ds[hsl] = vz
                if stage != 2:
                    x1o_do[hsl] = dma(
                        "ac", out3[:, 2 * hs:2 * hs + 2, :],
                        x1st[:, hsl * 2048:(hsl + 1) * 2048]
                        .rearrange("p (a d) -> p a d", a=2),
                        "do%d" % hsl)

            def ln1_step():
                while ln1_gens:
                    try:
                        next(ln1_gens[0])
                        return
                    except StopIteration:
                        ln1_gens.pop(0)

            def ln1_drain():
                while ln1_gens:
                    for _ in ln1_gens.pop(0):
                        pass

            if att_spans:
                emit_front(0)
                emit_rowmath(0)

            x1cast = [0] * NRANGE

            def emit_x1cast(r):
                x1cast[r] = dma(
                    "po", x1b[r * RANGE:(r + 1) * RANGE, :],
                    x1scr[r * RANGE:(r + 1) * RANGE, :],
                    "dxc%d" % r,
                    waits=(("ds0", 16 * (2 * r + 2)), ("ds1", 16 * (2 * r + 2))))

            for s_ in range(att_spans):
                if stage == 2 and s_ in (3, 5, 7):
                    emit_x1cast((s_ - 3) // 2)
                sl = s_ % 2
                F = front[s_]
                v_y = F["y"]
                v_silu = F["silu"]
                v_s2mm = F["s2mm"]
                v_statmm = F["s2mm"][7]

                if s_ + 1 < att_spans:
                    emit_transposes(s_ + 1)

                v_bcb = rb_vals[s_]
                v_y2b = [0] * 8
                v_ynbb = [0] * 8

                def yn_emit(g, F=F, sl=sl, v_bcb=v_bcb, v_y2b=v_y2b,
                            v_ynbb=v_ynbb, v_silu=v_silu):
                    yt = ynts[g % 2]
                    w = [("dv", v_bcb)]
                    if g >= 2:
                        w.append(("po", v_ynbb[g - 2]))
                    P.op("dv", lambda h, g=g, yt=yt: h.tensor_tensor(
                        yt, stat[:, g * SPAN:(g + 1) * SPAN],
                        bcb[:, 0:512], ALU.add),
                        waits=tuple(w))
                    P.op("dv", lambda h, g=g, yt=yt: h.tensor_tensor(
                        yt, yt, bcb[:, 512:1024], ALU.mult))
                    v_mul = P.mst("dv")
                    P.op("po", lambda h, g=g, yt=yt: h.tensor_scalar(
                        ynbb[:], yt,
                        lngc[:, g:g + 1], lnbc[:, g:g + 1],
                        op0=ALU.mult, op1=ALU.add),
                        waits=(("dv", v_mul),))
                    v_ynbb[g] = P.mst("po")
                    P.op("po", lambda h, g=g, sl=sl: h.tensor_tensor(
                        gtb[:, sl * 4096 + g * SPAN:sl * 4096 + (g + 1) * SPAN],
                        ynbb[:],
                        gtb[:, sl * 4096 + g * SPAN:sl * 4096 + (g + 1) * SPAN],
                        ALU.mult),
                        waits=(("ac", v_silu[g]),))
                    v_y2b[g] = P.mst("po")

                if s_ + 1 < att_spans:
                    emit_front(s_ + 1, yn_emit)
                    emit_rowmath(s_ + 1)
                else:
                    for g in range(8):
                        yn_emit(g)
                        ln1_step()

                for half in range(2):
                    hs = 2 * s_ + half
                    hsl = hs % 2
                    for ttl in range(2):
                        tt = half * 2 + ttl
                        gt = 4 * s_ + tt
                        slot = gt % 8
                        for oh in range(2):
                            bank = 2 + oh
                            for dyg in range(8):
                                w = (("po", v_y2b[dyg]),)
                                if dyg == 0:
                                    w = w + (("dwt_o", v_wt["o"]), bankf[bank])
                                P.op("pe", lambda h, tt=tt, oh=oh, dyg=dyg, bank=bank, sl=sl:
                                     h.matmul(
                                         ps[bank][:, 0:512],
                                         gtb[:, sl * 4096 + dyg * SPAN + tt * 128:
                                             sl * 4096 + dyg * SPAN + (tt + 1) * 128],
                                         wT["o"][:, dyg * 1024 + oh * 512:
                                                 dyg * 1024 + (oh + 1) * 512],
                                         start=(dyg == 0), stop=(dyg == 7)),
                                     waits=w)
                            v_wo = P.mst("pe")
                            P.op("dv", lambda h, hsl=hsl, ttl=ttl, oh=oh, slot=slot,
                                 bank=bank: h.tensor_tensor(
                                     x1st[:, hsl * 2048 + ttl * 1024 + oh * 512:
                                          hsl * 2048 + ttl * 1024 + (oh + 1) * 512],
                                     ps[bank][:, 0:512],
                                     xtm[:, slot * 1024 + oh * 512:
                                         slot * 1024 + (oh + 1) * 512],
                                     ALU.add),
                                 waits=(("pe", v_wo),
                                        ("ds%d" % hsl, x1slot_ds[hsl]),
                                        ("do%d" % hsl, x1o_do[hsl])))
                            bankf[bank] = ("dv", P.mst("dv"))
                        slot_free[slot] = ("dv", P.cnt["dv"])
                    ln1_gens.append(emit_ln1(hs, hsl, P.cnt["dv"]))
                v_wo_prev = P.cnt["pe"]
                xf_free[sl] = ("pe", front[s_]["gmm"][7])
                v_yn_prev = P.cnt["dv"]
                for gt in range(4 * s_ + 8, 4 * s_ + 12):
                    if gt < 4 * att_spans and gt not in xld:
                        load_tile(gt)

            pre_drain = {e: P.cnt.get(e, 0) for e in ("pe", "dv", "ac", "po")}
            ln1_drain()

            att_end = {e: P.cnt.get(e, 0) for e in ("pe", "dv", "ac", "po")}
            v_ds_all = P.cnt.get("ds", 0)

            P.mark_cut()
            if stage != 2:
                with nc.Block("attn") as blockA:
                    blockA.tensor(lambda h: P.replay("pe", h, sems, 0))
                    blockA.vector(lambda h: P.replay("dv", h, sems, 0))
                    blockA.scalar(lambda h: P.replay("ac", h, sems, 0))
                    blockA.gpsimd(lambda h: P.replay("po", h, sems, 0))
                    blockA.sync(lambda h: P.replay("sp", h, sems, 0))

        if stage != 2:
            dnames = (["dc", "db1", "dw", "dmeb", "dla", "dlb"]
                      + ["dwc_%s" % n for n in ("q", "v", "g", "o")]
                      + ["dwt_%s" % n for n in ("q", "v", "g", "o")]
                      + ["dx%d" % i for i in range(8)]
                      + ["ds0", "ds1", "do0", "do1"])
            dtot = [(k, P.cnt.get(k, 0)) for k in dnames]
            ctot = [(k, P.cnt.get(k, 0)) for k in ("pe", "dv", "ac")]

            def fin(h):
                for k, v in dtot + ctot:
                    if v > 0:
                        h.wait_ge(sems[k], v)
                if stage in ("setup", "look"):
                    h.dma_start(out=out[0:128, 0:8], in_=sinit[:]).then_inc(sems["do0"], 16)
                    h.wait_ge(sems["do0"], P.cnt.get("do0", 0) + 16)
            with nc.Block("fin") as blockF:
                blockF.sync(fin)
            return nc

        att_dma = [(k, P.cnt.get(k, 0)) for k in
                   (["dc", "db1", "dw", "dla", "dlb", "ds0", "ds1", "do0", "do1"]
                    + ["dwc_%s" % n for n in ("q", "v", "g", "o")]
                    + ["dwt_%s" % n for n in ("q", "v", "g", "o")]
                    + ["dx%d" % i for i in range(8)])
                   if P.cnt.get(k, 0) > 0]
        barrier = tuple([("pe", att_end["pe"]), ("dv", att_end["dv"]),
                         ("ac", att_end["ac"]), ("po", att_end["po"])] + att_dma)
        with ExitStack() as moe:
            x1fmb = sb(moe, "x1fmb", [128, 2 * 8 * 1024], BF16)
            wbuf = [sb(moe, "wb%d" % i, [128, 16384], BF16) for i in range(3)]
            acc = sb(moe, "acc", [128, 8 * 1024], F32)
            hb = sb(moe, "hb", [128, 16 * 512], BF16)
            sqm = sb(moe, "sqm", [128, 512], F32)
            o2st = sb(moe, "o2st", [128, 2048], F32)
            gts = sb(moe, "gts", [128, 8], F32)
            tt1 = sb(moe, "tt1", [128, 8], F32)
            tt2 = sb(moe, "tt2", [128, 8], F32)
            cwsb = sb(moe, "cwsb", [128, 2 * 64], F32)
            lgs = sb(moe, "lgs", [128, 64], F32)
            cwF = sb(moe, "cwF", [NE, 1024], BF16)

            order = []
            for r in range(NRANGE):
                es = list(range(NE)) if r % 2 == 0 else list(range(NE - 1, -1, -1))
                for e in es:
                    order.append((r, e))
            NK = len(order)
            bufW1 = [0] * NK
            bufW2 = [0] * NK
            newload = [False] * NK
            cur1, cur2, nxt = 0, 1, 2
            for k in range(NK):
                if k == 0:
                    newload[0] = True
                    bufW1[0], bufW2[0] = cur1, cur2
                elif order[k][1] == order[k - 1][1]:
                    bufW1[k], bufW2[k] = bufW1[k - 1], bufW2[k - 1]
                    cur1, cur2 = bufW1[k], bufW2[k]
                else:
                    newload[k] = True
                    o1, o2 = cur1, cur2
                    cur1 = nxt
                    cur2 = o1
                    nxt = o2
                    bufW1[k], bufW2[k] = cur1, cur2

            w13 = w1.rearrange("e (kg p) j -> e p kg j", p=128)
            w23 = w2.rearrange("e (jg p) o -> e p jg o", p=128)

            def load_w1(e, buf, waits):
                v = dma("po", wbuf[buf][:].rearrange("p (kg j) -> p kg j", kg=8),
                        w13[e], "dwb%d" % buf, waits=waits)
                return ("dwb%d" % buf, v)

            def load_w2(e, buf, waits):
                v = dma("po", wbuf[buf][:].rearrange("p (jg o) -> p jg o", jg=16),
                        w23[e], "dwb%d" % buf, waits=waits)
                return ("dwb%d" % buf, v)

            v_w1ld = [None] * NK
            v_w2ld = [None] * NK
            nl = [k for k in range(1, NK) if newload[k]]
            w1_after = {}
            for j in range(1, len(nl)):
                kj = nl[j]
                tb = bufW1[kj]
                kprev = max(k2 for k2 in range(kj) if bufW2[k2] == tb)
                w1_after.setdefault(kprev, []).append(kj)

            v_xT = [0] * NRANGE
            v_cwt = [0] * 8
            v_hdone = 0
            v_edone = 0
            v_hbfree = 0
            v_accinit = [0] * NRANGE
            v_ln2 = 0
            v_moe_prev = 0
            v_fmb_user = {0: 0, 1: 0}
            v_cwall = [0] * NRANGE
            v_biasmm = [0]
            o2_do = {0: 0, 1: 0}

            def emit_head_a(r):
                sl = r % 2
                fmb = x1fmb[:, sl * 8192:(sl + 1) * 8192]
                c_sem = "dxc%d" % r
                t0s, t1s = "drb%d" % (2 * (r % 2)), "drb%d" % (2 * (r % 2) + 1)
                for kg in range(8):
                    wl = ()
                    if kg == 0:
                        wl = ((c_sem, x1cast[r]), ("pe", v_fmb_user[sl]))
                        if r == 0:
                            wl = wl + (("pe", front[NSPAN - 1]["vmm"][7]),)
                    dma("sp", fmb[:, kg * 1024:(kg + 1) * 1024],
                        x1b[r * RANGE:(r + 1) * RANGE, kg * 128:(kg + 1) * 128],
                        t0s if kg % 2 == 0 else t1s, waits=wl, transpose=True)
                v_xT[r] = ((t0s, P.cnt[t0s]), (t1s, P.cnt[t1s]))

            def emit_head_b(r):
                sl = r % 2
                fmb = x1fmb[:, sl * 8192:(sl + 1) * 8192]
                for tt in range(8):
                    bank = 6 + (tt % 2)
                    for kg in range(8):
                        w = ()
                        if kg == 0:
                            w = v_xT[r] + (("dmeb", v_meb), bankf[bank])
                        P.op("pe", lambda h, tt=tt, kg=kg, bank=bank, fmb=fmb: h.matmul(
                            ps[bank][:, 0:8],
                            fmb[:, kg * 1024 + tt * 128:kg * 1024 + (tt + 1) * 128],
                            gwb[:, kg * 8:(kg + 1) * 8],
                            start=(kg == 0), stop=(kg == 7)),
                            waits=w)
                    v_lg = P.mst("pe")
                    P.op("dv", lambda h, tt=tt, bank=bank: h.tensor_copy(
                        lgs[:, tt * 8:(tt + 1) * 8], ps[bank][:, 0:8]),
                        waits=(("pe", v_lg),))
                    bankf[bank] = ("dv", P.mst("dv"))
            def emit_cw_chain(r, tt):
                sl = r % 2
                P.op("dv", lambda h, tt=tt: h.reduce_max(
                    rcol[:, 0:1], lgs[:, tt * 8:(tt + 1) * 8], AX.X,
                    negate=True))
                v_nm = P.mst("dv")
                P.op("ac", lambda h, tt=tt: h.activation(
                    gts[:], lgs[:, tt * 8:(tt + 1) * 8], AF.Exp,
                    bias=rcol[:, 0:1]),
                    waits=(("dv", v_nm),))
                v_exp = P.mst("ac")
                P.op("dv", lambda h: h.reduce_sum(rcol[:, 1:2], gts[:], AX.X),
                     waits=(("ac", v_exp),))
                P.op("dv", lambda h: h.reciprocal(rcol[:, 2:3], rcol[:, 1:2]))
                P.op("dv", lambda h: h.tensor_scalar_mul(
                    gts[:], gts[:], rcol[:, 2:3]))
                P.op("dv", lambda h: h.reduce_max(rcol[:, 3:4], gts[:], AX.X))
                P.op("dv", lambda h: h.tensor_scalar(
                    tt1[:], gts[:], rcol[:, 3:4], None, op0=ALU.is_ge))
                P.op("dv", lambda h: h.scalar_tensor_tensor(
                    tt2[:], tt1[:], -BIG, gts[:], op0=ALU.mult, op1=ALU.add))
                P.op("dv", lambda h: h.reduce_max(rcol[:, 4:5], tt2[:], AX.X))
                P.op("dv", lambda h: h.tensor_scalar(
                    tt1[:], gts[:], rcol[:, 4:5], None, op0=ALU.is_ge))
                P.op("dv", lambda h: h.tensor_tensor(
                    rcol[:, 5:6], rcol[:, 3:4], rcol[:, 4:5], ALU.add))
                P.op("dv", lambda h: h.tensor_scalar_add(
                    rcol[:, 5:6], rcol[:, 5:6], 1e-8))
                P.op("dv", lambda h: h.reciprocal(rcol[:, 6:7], rcol[:, 5:6]))
                P.op("dv", lambda h: h.tensor_tensor(
                    gts[:], gts[:], tt1[:], ALU.mult))
                P.op("dv", lambda h, tt=tt, sl=sl: h.tensor_scalar_mul(
                    cwsb[:, sl * 64 + tt * 8:sl * 64 + (tt + 1) * 8],
                    gts[:], rcol[:, 6:7]))
                v_cwt[tt] = P.mst("dv")

            def emit_head_c(r):
                sl = r % 2
                for tt in range(8):
                    bank = 6 + (tt % 2)
                    P.op("pe", lambda h, tt=tt, bank=bank, sl=sl: h.transpose(
                        ps[bank][0:8, 0:128],
                        cwsb[:, sl * 64 + tt * 8:sl * 64 + (tt + 1) * 8], idfs[:]),
                        waits=(("dv", v_cwt[tt]), bankf[bank]))
                    v_p = P.mst("pe")
                    P.op("dv", lambda h, tt=tt, bank=bank: h.tensor_copy(
                        cwF[0:NE, tt * 128:(tt + 1) * 128],
                        ps[bank][0:NE, 0:128]),
                        waits=(("pe", v_p), ("pe", v_biasmm[0])))
                    bankf[bank] = ("dv", P.mst("dv"))
                v_cwall[r] = P.cnt["dv"]

            def emit_ln2_stats(tt):
                a0 = tt * 1024
                v_acc = P.cnt["dv"]
                P.op("po", lambda h, a0=a0: h.tensor_tensor(
                    sqm[:], acc[:, a0:a0 + 512], acc[:, a0:a0 + 512],
                    ALU.mult),
                    waits=(("dv", v_acc),))
                v_sq1 = P.mst("po")
                P.op("dv", lambda h, a0=a0, tt=tt: h.reduce_sum(
                    rcol[:, tt:tt + 1], acc[:, a0:a0 + 1024], AX.X,
                    negate=True))
                P.op("dv", lambda h, tt=tt: h.tensor_scalar_mul(
                    rcol[:, tt:tt + 1], rcol[:, tt:tt + 1], R1024))
                P.op("dv", lambda h: h.reduce_sum(gts[:, 0:1], sqm[:], AX.X),
                     waits=(("po", v_sq1),))
                v_r1 = P.cnt["dv"]
                P.op("po", lambda h, a0=a0: h.tensor_tensor(
                    sqm[:], acc[:, a0 + 512:a0 + 1024],
                    acc[:, a0 + 512:a0 + 1024], ALU.mult),
                    waits=(("dv", v_r1),))
                v_sq2 = P.mst("po")
                P.op("dv", lambda h: h.reduce_sum(gts[:, 1:2], sqm[:], AX.X),
                     waits=(("po", v_sq2),))
                P.op("dv", lambda h: h.tensor_tensor(
                    gts[:, 0:1], gts[:, 0:1], gts[:, 1:2], ALU.add))
                P.op("dv", lambda h: h.tensor_scalar_mul(
                    gts[:, 0:1], gts[:, 0:1], R1024))
                P.op("dv", lambda h, tt=tt: h.tensor_tensor(
                    gts[:, 1:2], rcol[:, tt:tt + 1], rcol[:, tt:tt + 1],
                    ALU.mult))
                P.op("dv", lambda h, tt=tt: h.tensor_tensor(
                    rcol2[:, tt:tt + 1], gts[:, 0:1], gts[:, 1:2],
                    ALU.subtract))

            dsf = tuple((k, P.cnt[k]) for k in ("ds0", "ds1") if P.cnt.get(k, 0))
            v_w1ld[0] = load_w1(order[0][1], bufW1[0],
                                (("pe", att_end["pe"]),))
            v_w2ld[0] = load_w2(order[0][1], bufW2[0],
                                (("dv", pre_drain["dv"]),))
            emit_x1cast(NRANGE - 1)
            emit_head_a(0)
            P.op("dv", None, waits=(("po", att_end["po"]),) + dsf)
            P.op("ac", None,
                 waits=(("po", att_end["po"]), ("dv", att_end["dv"])) + dsf)
            P.op("po", None,
                 waits=(("pe", att_end["pe"]), ("dv", att_end["dv"])))
            if nl:
                v_w1ld[nl[0]] = load_w1(order[nl[0]][1], bufW1[nl[0]], ())
            emit_head_b(0)

            for r in range(NRANGE):
                sl = r % 2
                fmb = x1fmb[:, sl * 8192:(sl + 1) * 8192]
                aw = (("ds0", 16 * (2 * r + 2)), ("ds1", 16 * (2 * r + 2)),
                      ("dv", v_ln2))
                if r == 0:
                    aw = aw + barrier
                v_accinit[r] = dma(
                    "sp", acc[:].rearrange("p (a d) -> p a d", a=8),
                    x1s3[:, 8 * r:8 * r + 8, :], "dr",
                    waits=aw)
                for tt8 in range(8):
                    P.op("dv", lambda h, tt8=tt8: h.tensor_scalar_mul(
                        acc[:, tt8 * 1024:(tt8 + 1) * 1024],
                        acc[:, tt8 * 1024:(tt8 + 1) * 1024], 2.0),
                        waits=(("dr", v_accinit[r]),))
                v_acc2 = P.mst("dv")

                base_k = 8 * r
                for ki in range(NE):
                    k = base_k + ki
                    (rr, e) = order[k]
                    for sp2 in range(2):
                        v_h = [0] * 16
                        v_relu = [0] * 16
                        for jg in range(16):
                            bank = jg % 2
                            for kg in range(8):
                                w = ()
                                if kg == 0:
                                    w = (v_w1ld[k], bankf[bank]) + v_xT[r]
                                P.op("pe", lambda h, jg=jg, kg=kg, sp2=sp2, bank=bank,
                                     wb=wbuf[bufW1[k]], fmb=fmb: h.matmul(
                                         ps[bank][:, 0:512],
                                         wb[:, kg * 2048 + jg * 128:kg * 2048 + (jg + 1) * 128],
                                         fmb[:, kg * 1024 + sp2 * 512:kg * 1024 + (sp2 + 1) * 512],
                                         start=(kg == 0), stop=(kg == 7)),
                                     waits=w)
                            v_h[jg] = P.mst("pe")
                            P.op("ac", lambda h, jg=jg, e=e, bank=bank: h.activation(
                                hb[:, jg * 512:(jg + 1) * 512], ps[bank][:, 0:512],
                                AF.Relu, bias=b1T[:, jg * 8 + e:jg * 8 + e + 1]),
                                waits=(("pe", v_h[jg]), ("pe", v_hbfree)))
                            v_relu[jg] = P.mst("ac")
                            bankf[bank] = ("ac", v_relu[jg])
                        v_hdone = P.cnt["pe"]
                        if sp2 == 1 and k + 1 < NK and newload[k + 1]:
                            v_w2ld[k + 1] = load_w2(order[k + 1][1], bufW2[k + 1],
                                                    (("pe", v_hdone),))
                        for tl in range(4):
                            for oh in range(2):
                                bank = 2 + (tl * 2 + oh) % 4
                                gt = sp2 * 4 + tl
                                for jg in range(16):
                                    w2w = []
                                    if jg == 0:
                                        w2w = [v_w2ld[k], bankf[bank],
                                               ("ac", v_relu[3])]
                                    elif jg in (4, 8, 12):
                                        w2w = [("ac", v_relu[min(jg + 3, 15)])]
                                    P.op("pe", lambda h, jg=jg, tl=tl, oh=oh, bank=bank,
                                         wb=wbuf[bufW2[k]]: h.matmul(
                                             ps[bank][:, 0:512],
                                             hb[:, jg * 512 + tl * 128:jg * 512 + (tl + 1) * 128],
                                             wb[:, jg * 1024 + oh * 512:jg * 1024 + (oh + 1) * 512],
                                             start=(jg == 0), stop=(jg == 15)),
                                         waits=tuple(w2w))
                                v_w2g = P.mst("pe")
                                if sp2 == 0:
                                    if ki == 3 and r + 1 < NRANGE:
                                        emit_cw_chain(r + 1, tl * 2 + oh)
                                    elif ki == 0 and r == 0:
                                        emit_cw_chain(0, tl * 2 + oh)
                                P.op("dv", lambda h, gt=gt, oh=oh, e=e, bank=bank, sl=sl:
                                     h.scalar_tensor_tensor(
                                         acc[:, gt * 1024 + oh * 512:gt * 1024 + (oh + 1) * 512],
                                         ps[bank][:, 0:512],
                                         cwsb[:, sl * 64 + gt * 8 + e:sl * 64 + gt * 8 + e + 1],
                                         acc[:, gt * 1024 + oh * 512:gt * 1024 + (oh + 1) * 512],
                                         op0=ALU.mult, op1=ALU.add),
                                     waits=(("pe", v_w2g),))
                                bankf[bank] = ("dv", P.mst("dv"))
                                if ki == NE - 1 and oh == 1:
                                    emit_ln2_stats(gt)
                        v_hbfree = P.cnt["pe"]
                    v_edone = P.cnt["pe"]
                    v_moe_prev = v_edone
                    for kj in w1_after.get(k, ()):
                        v_w1ld[kj] = load_w1(order[kj][1], bufW1[kj],
                                             (("pe", v_edone),))
                    if k + 1 < NK and not newload[k + 1]:
                        v_w1ld[k + 1] = v_w1ld[k]
                        v_w2ld[k + 1] = v_w2ld[k]
                    if r == 0 and ki == 1:
                        emit_head_c(0)
                    if ki == 2:
                        for tt in range(8):
                            for oh in range(2):
                                bank = 6 + ((2 * tt + oh) % 2)
                                P.op("pe", lambda h, tt=tt, oh=oh, bank=bank:
                                     h.matmul(
                                         ps[bank][:, 0:512],
                                         cwF[0:NE, tt * 128:(tt + 1) * 128],
                                         b2b[0:NE, oh * 512:(oh + 1) * 512],
                                         start=True, stop=True),
                                     waits=(("dmeb", v_meb), bankf[bank]))
                                v_p = P.mst("pe")
                                P.op("dv", lambda h, tt=tt, oh=oh, bank=bank:
                                     h.tensor_tensor(
                                         acc[:, tt * 1024 + oh * 512:tt * 1024 + (oh + 1) * 512],
                                         ps[bank][:, 0:512],
                                         acc[:, tt * 1024 + oh * 512:tt * 1024 + (oh + 1) * 512],
                                         ALU.add),
                                     waits=(("pe", v_p),))
                                bankf[bank] = ("dv", P.mst("dv"))
                        v_biasmm[0] = P.cnt["pe"]
                    if r + 1 < NRANGE:
                        if ki == 1:
                            emit_head_a(r + 1)
                        elif ki == 2:
                            emit_head_b(r + 1)
                        elif ki == 4:
                            emit_head_c(r + 1)
                v_fmb_user[sl] = v_edone

                P.op("dv", lambda h: h.tensor_scalar_add(
                    rcol2[:], rcol2[:], LN_EPS))
                v_v = P.mst("dv")
                P.op("ac", lambda h: h.activation(rcol2[:], rcol2[:], AF.Sqrt),
                     waits=(("dv", v_v),))
                v_sdc = P.mst("ac")
                P.op("dv", lambda h: h.reciprocal(rcol2[:], rcol2[:]),
                     waits=(("ac", v_sdc),))
                for tt in range(8):
                    a0 = tt * 1024
                    o2c = (tt % 2) * 1024
                    P.op("dv", lambda h, o2c=o2c, a0=a0, tt=tt: h.tensor_scalar(
                        o2st[:, o2c:o2c + 1024],
                        acc[:, a0:a0 + 1024], rcol[:, tt:tt + 1],
                        rcol2[:, tt:tt + 1],
                        op0=ALU.add, op1=ALU.mult),
                        waits=(("do0", o2_do[0]),))
                    v_o2 = P.mst("dv")
                    if tt % 2 == 1:
                        vdo = dma("sp", out3[:, 8 * r + tt - 1:8 * r + tt + 1, :],
                                  o2st[:].rearrange("p (a d) -> p a d", a=2),
                                  "do0",
                                  waits=(("dv", v_o2),))
                        o2_do[0] = vdo
                v_ln2 = P.cnt["dv"]

            P.op("sp", None, waits=(("do0", P.cnt.get("do0", 0)),
                                    ("do1", P.cnt.get("do1", 0))))


            with nc.Block("moe") as blockB:
                blockB.tensor(lambda h: P.replay("pe", h, sems, None))
                blockB.vector(lambda h: P.replay("dv", h, sems, None))
                blockB.scalar(lambda h: P.replay("ac", h, sems, None))
                blockB.gpsimd(lambda h: P.replay("po", h, sems, None))
                blockB.sync(lambda h: P.replay("sp", h, sems, None))

    return nc


_CACHED = {}


def _get_program(stage):
    if stage not in _CACHED:
        _CACHED[stage] = build_program(stage)
    return _CACHED[stage]


def kernel(L0_attn_x, L0_attn_sin, L0_attn_cos, L0_attn_decay, L0_attn_Wq,
           L0_attn_Wv, L0_attn_Wo, L0_attn_Wg, L0_attn_ln_g, L0_attn_ln_b,
           L0_ffn_gate_W, L0_ffn_W1, L0_ffn_b1, L0_ffn_W2, L0_ffn_b2):
    x = np.ascontiguousarray(L0_attn_x, dtype=np.float32)
    idfv = np.eye(128, dtype=np.float32)
    idbv = np.eye(128).astype(ml_dtypes.bfloat16)
    erv = np.broadcast_to(np.arange(LCH - 1, -1, -1, dtype=np.float32),
                          (NH, LCH)).copy()

    nc = _get_program(STAGE)

    shared = dict(
        wq=np.ascontiguousarray(L0_attn_Wq, dtype=np.float32),
        wv=np.ascontiguousarray(L0_attn_Wv, dtype=np.float32),
        wg=np.ascontiguousarray(L0_attn_Wg, dtype=np.float32),
        wo=np.ascontiguousarray(L0_attn_Wo, dtype=np.float32),
        dec=np.ascontiguousarray(L0_attn_decay, dtype=np.float32),
        lng=np.ascontiguousarray(L0_attn_ln_g, dtype=np.float32),
        lnb=np.ascontiguousarray(L0_attn_ln_b, dtype=np.float32),
        gwt=np.ascontiguousarray(L0_ffn_gate_W, dtype=np.float32),
        w1=np.ascontiguousarray(L0_ffn_W1, dtype=np.float32),
        b1=np.ascontiguousarray(L0_ffn_b1, dtype=np.float32),
        w2=np.ascontiguousarray(L0_ffn_W2, dtype=np.float32),
        b2=np.ascontiguousarray(L0_ffn_b2, dtype=np.float32),
        idf=idfv, idb=idbv, erow=erv,
    )
    in_maps = []
    for c in range(8):
        b, qt = divmod(c, 4)
        xc = np.ascontiguousarray(x[b, qt * TPC:(qt + 1) * TPC])
        xl = np.zeros((LOOK, D), dtype=np.float32)
        if qt > 0:
            n = qt * TPC
            xl[LOOK - n:] = x[b, 0:n]
        m = dict(shared)
        m["xin"] = xc
        m["xlk"] = xl
        in_maps.append(m)

    res = run_bass_kernel_spmd(nc, in_maps, core_ids=list(range(8)))
    outv = np.empty((B, L, D), dtype=np.float32)
    for c in range(8):
        b, qt = divmod(c, 4)
        outv[b, qt * TPC:(qt + 1) * TPC] = res.results[c]["out"]
    return outv
